# version 20
# baseline (speedup 1.0000x reference)
from contextlib import ExitStack

import numpy as np

import concourse.bass as bass
import concourse.mybir as mybir
from concourse.bass_utils import run_bass_kernel_spmd

F32 = mybir.dt.float32
BF16 = mybir.dt.bfloat16
F16 = mybir.dt.float16
I32 = mybir.dt.int32
U16 = mybir.dt.uint16
AF = mybir.ActivationFunctionType
ALU = mybir.AluOpType

DM = 2048
NF = DM // 128
II = 5632
SHIFT = 16.0
MASKBIG = 60000.0

ENGINES = ("sync", "scalar", "vector", "tensor", "gpsimd")
COMPUTE_SEMS = ("pe", "act", "dve", "pool", "ccx", "cck", "ccg", "ccd", "cco")
SINGLE_DMA_SEMS = tuple(f"dq{i}" for i in range(12)) + (
    "gkv", "go", "wkc", "wvc", "wqc", "woc",
    "wq0", "wq1", "wq2", "wq3") + tuple(f"sg{i}" for i in range(10))
MULTI_DMA_SEMS = ("ld0", "ld1", "sn0", "sn1", "sxn", "skv0", "skv1", "kfm", "vtm",
                  "mst0", "mst1", "mld0", "mld1", "mld2", "mld3",
                  "zs0", "zs1", "zl0", "zl1",
                  "xg0", "xg1", "ost", "xp0", "xp1", "ou0", "ou1",
                  "so0", "so1", "wb0", "wb1", "xfm", "wts", "os0", "os1") + \
                 tuple(f"dn{i}" for i in range(8))


def _flat(after):
    out = []
    for t in after:
        if t is None:
            continue
        if isinstance(t, tuple) and len(t) == 2 and isinstance(t[0], str):
            out.append(t)
        elif isinstance(t, (list, tuple)):
            out.extend(_flat(t))
    return out


class Prog:
    def __init__(self, nc):
        self.nc = nc
        self.qs = {e: [] for e in ENGINES}
        self.semnames = list(COMPUTE_SEMS) + list(SINGLE_DMA_SEMS) + \
            list(MULTI_DMA_SEMS)
        self.cnt = {s: 0 for s in self.semnames}
        self.sat = {e: {s: 0 for s in self.semnames} for e in ENGINES}
        self.sem = {}

    def alloc_sems(self, ctx):
        for s in self.semnames:
            self.sem[s] = ctx.enter_context(self.nc.semaphore(s))

    def op(self, eng, fn, after=(), sem="auto", inc=None):
        if sem == "auto":
            sem = {"tensor": "pe", "scalar": "act", "vector": "dve",
                   "gpsimd": "pool"}[eng]
        waits = []
        for s, v in _flat(after):
            if v > self.sat[eng][s]:
                self.sat[eng][s] = v
                waits.append((s, v))
        amt = inc if inc is not None else (1 if sem in COMPUTE_SEMS else 16)
        val = None
        if sem is not None:
            self.cnt[sem] += amt
            val = self.cnt[sem]

        def clo(e, waits=waits, fn=fn, sem=sem, amt=amt):
            for s, v in waits:
                e.wait_ge(self.sem[s], v)
            ins = fn(e)
            if sem is not None:
                ins.then_inc(self.sem[sem], amt)

        self.qs[eng].append(clo)
        return (sem, val) if sem is not None else None

    def wait(self, eng, after):
        waits = []
        for s, v in _flat(after):
            if v > self.sat[eng][s]:
                self.sat[eng][s] = v
                waits.append((s, v))
        if waits:
            def clo(e, waits=waits):
                for s, v in waits:
                    e.wait_ge(self.sem[s], v)
            self.qs[eng].append(clo)

    def barrier(self, exclude=()):
        toks = [(s, self.cnt[s]) for s in self.semnames if s not in exclude]
        for e in ENGINES:
            self.wait(e, toks)

    def emit(self):
        with self.nc.Block() as block:
            @block.sync
            def _(e):
                for f in self.qs["sync"]:
                    f(e)

            @block.scalar
            def _(e):
                for f in self.qs["scalar"]:
                    f(e)

            @block.vector
            def _(e):
                for f in self.qs["vector"]:
                    f(e)

            @block.tensor
            def _(e):
                for f in self.qs["tensor"]:
                    f(e)

            @block.gpsimd
            def _(e):
                for f in self.qs["gpsimd"]:
                    f(e)


def build_nc(S=4096):
    TQ = S * 2 // 8
    assert TQ % 512 == 0
    NTT = TQ // 128
    NQB = TQ // 512
    NQT = S // 512
    NOFF = 4 * (NQT - 1) + 4
    GUC2 = 64
    DNC2 = 176
    NIT = 88
    NOT_ = 16
    DNI = 704

    nc = bass.Bass("TRN2", target_bir_lowering=False, num_devices=8)

    x_own = nc.dram_tensor("x_own", [TQ, DM], F32, kind="ExternalInput").ap()
    wq_in = nc.dram_tensor("wq_in", [DM, 512], F32, kind="ExternalInput").ap()
    wk_in = nc.dram_tensor("wk_in", [DM, 512], F32, kind="ExternalInput").ap()
    wv_in = nc.dram_tensor("wv_in", [DM, 512], F32, kind="ExternalInput").ap()
    wo_in = nc.dram_tensor("wo_in", [DM, DM], F32, kind="ExternalInput").ap()
    gup = nc.dram_tensor("gup", [128, NIT * GUC2], U16, kind="ExternalInput").ap()
    dnp = nc.dram_tensor("dnp", [128, NOT_ * DNC2], U16, kind="ExternalInput").ap()
    gusv = nc.dram_tensor("gusv", [128, 2 * NIT], F16, kind="ExternalInput").ap()
    dnsv = nc.dram_tensor("dnsv", [128, 11 * NOT_], F16, kind="ExternalInput").ap()
    cvin = nc.dram_tensor("cvin", [128, 8], F32, kind="ExternalInput").ap()
    meta = nc.dram_tensor("meta", [1, 2], I32, kind="ExternalInput").ap()
    out_own = nc.dram_tensor("out_own", [TQ, DM], F32, kind="ExternalOutput").ap()

    dr = lambda n, s, d: nc.dram_tensor(n, s, d, kind="Internal").ap()
    xn_tm = dr("xn_tm", [TQ, DM], BF16)
    w_dec_gu = dr("w_dec_gu", [2 * II, DM // 8], BF16)
    w_dec_dn = dr("w_dec_dn", [DM, DNI], BF16)
    send_xn = dr("send_xn", [DM, TQ], BF16)
    gxn = dr("gxn", [4 * DM, TQ], BF16)
    send_kv = dr("send_kv", [1024, TQ], BF16)
    g_kv = dr("g_kv", [4096, TQ], BF16)
    recv_kv = dr("recv_kv", [1024, TQ], BF16)
    send_wgu = dr("send_wgu", [DM // 8, 2 * II], BF16)
    g_wgu = nc.dram_tensor("g_wgu", [DM, 2 * II], BF16, kind="Internal",
                           addr_space="Shared").ap()
    send_wdn = dr("send_wdn", [DNI, DM], BF16)
    g_wdn = nc.dram_tensor("g_wdn", [II, DM], BF16, kind="Internal",
                           addr_space="Shared").ap()
    m_dram = dr("m_dram", [128, NOFF * 512], F16)
    z_dram = dr("z_dram", [1, 2 * 512], F32)
    w_dram = dr("w_dram", [DM, DM], BF16)
    send_o = dr("send_o", [512, S], BF16)
    g_o = dr("g_o", [2048, S], BF16)
    recv_o = dr("recv_o", [2048, TQ], BF16)

    ctx = ExitStack()
    sb = lambda n, s, d: ctx.enter_context(nc.sbuf_tensor(n, s, d))

    CONSTS = [0.0, 1.0, -float(SHIFT), 1.0 / DM, 1e-5, 1e-6]
    cpool = sb("cpool", [128, len(CONSTS)], F32)
    for i, cv in enumerate(CONSTS):
        nc.const_aps.register(F32, cv, cpool[:, i:i + 1])
    cp = sb("cp", [128, 8], F32)
    meta_sb = sb("meta_sb", [1, 2], I32)
    xn_fm = sb("xn_fm", [128, NF * TQ], BF16)
    id16 = sb("id16", [128, 128], F16)
    ones_col = sb("ones_col", [128, 1], BF16)
    ones_row = sb("ones_row", [1, 128], BF16)
    onesf16 = sb("onesf16", [128, 128], F16)
    amt16 = sb("amt16", [128, DNI], U16)
    gusf = sb("gusf", [128, 2 * NIT], F32)
    dnsf = sb("dnsf", [128, 11 * NOT_], F32)
    psP = [ctx.enter_context(nc.psum_tensor(f"psP{i}", [128, 1024], F32))
           for i in range(4)]
    psA = [psP[i // 2][:, (i % 2) * 512:(i % 2 + 1) * 512] for i in range(8)]

    prog = Prog(nc)
    prog.alloc_sems(ctx)
    P = prog.op
    group4 = [[0, 1, 2, 3], [4, 5, 6, 7]]
    group8 = [list(range(8))]

    tok_consts = []
    for i, cv in enumerate(CONSTS):
        tok_consts.append(P("gpsimd", lambda e, i=i, cv=cv: e.memset(
            cpool[:, i:i + 1], cv)))
    t_on1 = P("gpsimd", lambda e: e.memset(ones_col[:, :], 1.0))
    t_on2 = P("gpsimd", lambda e: e.memset(ones_row[:, :], 1.0))
    t_on3 = P("gpsimd", lambda e: e.memset(onesf16[:, :], 1.0))
    t_id = P("gpsimd", lambda e: e.affine_select(
        id16[:, :], onesf16[:, :], [[1, 128]], ALU.is_equal, 0.0,
        base=0, channel_multiplier=-1), after=[t_on3])
    t_amt = P("gpsimd", lambda e: e.iota(
        amt16[:, :], [[0, DNI // 4], [4, 4]], base=0, channel_multiplier=0,
        allow_small_or_imprecise_dtypes=True))
    tok_consts += [t_on1, t_on2, t_id, t_amt]

    sP = ExitStack()
    sPb = lambda n, s, d: sP.enter_context(nc.sbuf_tensor(n, s, d))
    sw_gu = sPb("sw_gu", [128, NIT * GUC2], U16)
    sw_dn = sPb("sw_dn", [128, NOT_ * DNC2], U16)
    gus16 = sPb("gus16", [128, 2 * NIT], F16)
    dns16 = sPb("dns16", [128, 11 * NOT_], F16)
    P("scalar", lambda e: e.dma_start(sw_gu[:, :], gup), sem="ld1")
    P("scalar", lambda e: e.dma_start(sw_dn[:, :], dnp), sem="ld1")
    P("scalar", lambda e: e.dma_start(gus16[:, :], gusv), sem="ld1")
    t_ld1 = P("scalar", lambda e: e.dma_start(dns16[:, :], dnsv), sem="ld1")

    s0 = ExitStack()
    s0b = lambda n, s, d: s0.enter_context(nc.sbuf_tensor(n, s, d))
    x_sb = s0b("x_sb", [128, NTT * DM], F32)
    xn_ping = s0b("xn_ping", [128, 2 * DM], BF16)
    dump0 = s0b("dump0", [128, DM], F32)
    stat = s0b("stat", [128, 6 * NTT], F32)

    P("sync", lambda e: e.dma_start(cp[:, :], cvin), sem="ld0")
    P("sync", lambda e: e.dma_start(meta_sb[:, :], meta), sem="ld0")
    t_ld0 = P("sync", lambda e: e.dma_start(
        bass.AP(x_sb, 0, [[NTT * DM, 128], [DM, NTT], [1, DM]]),
        bass.AP(x_own.tensor, 0, [[DM, 128], [128 * DM, NTT], [1, DM]])),
        sem="ld0")

    t_xn_st = []
    sn_last = [None, None]
    prev_sq = None
    for i in range(NTT):
        c0 = 6 * i
        xt = x_sb[:, i * DM:(i + 1) * DM]
        ts1 = P("vector", lambda e, xt=xt, c0=c0: e.reduce_sum(
            stat[:, c0:c0 + 1], xt, mybir.AxisListType.X), after=[t_ld0])
        ts2 = P("scalar", lambda e, xt=xt, c0=c0: e.activation(
            dump0[:, :], xt, AF.Square, accum_out=stat[:, c0 + 1:c0 + 2]),
            after=[t_ld0, prev_sq] + tok_consts)
        ts3 = P("vector", lambda e, c0=c0: e.tensor_single_scalar(
            stat[:, c0 + 2:c0 + 3], stat[:, c0:c0 + 1], 1.0 / DM, ALU.mult),
            after=[ts1])
        ts4 = P("vector", lambda e, c0=c0: e.tensor_tensor(
            stat[:, c0 + 3:c0 + 4], stat[:, c0 + 2:c0 + 3],
            stat[:, c0 + 2:c0 + 3], ALU.mult), after=[ts3])
        ts5 = P("vector", lambda e, c0=c0: e.tensor_single_scalar(
            stat[:, c0 + 4:c0 + 5], stat[:, c0 + 1:c0 + 2], 1.0 / DM, ALU.mult),
            after=[ts2])
        ts6 = P("vector", lambda e, c0=c0: e.tensor_tensor(
            stat[:, c0 + 3:c0 + 4], stat[:, c0 + 4:c0 + 5],
            stat[:, c0 + 3:c0 + 4], ALU.subtract), after=[ts4, ts5])
        ts7 = P("scalar", lambda e, c0=c0: e.activation(
            stat[:, c0 + 5:c0 + 6], stat[:, c0 + 3:c0 + 4], AF.Sqrt,
            bias=1e-5, scale=1.0), after=[ts6])
        prev_sq = ts7
        ts8 = P("vector", lambda e, c0=c0: e.reciprocal(
            stat[:, c0 + 5:c0 + 6], stat[:, c0 + 5:c0 + 6]), after=[ts7])
        pp = i % 2
        ts9 = P("vector", lambda e, xt=xt, c0=c0, pp=pp: e.tensor_scalar(
            xn_ping[:, pp * DM:(pp + 1) * DM], xt,
            stat[:, c0 + 2:c0 + 3], stat[:, c0 + 5:c0 + 6],
            ALU.subtract, ALU.mult),
            after=[ts8, t_xn_st[i - 2] if i >= 2 else None])
        tok = P("sync", lambda e, i=i, pp=pp: e.dma_start(
            xn_tm[i * 128:(i + 1) * 128, :], xn_ping[:, pp * DM:(pp + 1) * DM]),
            after=[ts9], sem=f"sn{pp}")
        sn_last[pp] = tok
        t_xn_st.append(tok)

    for fi in range(NF):
        P("scalar", lambda e, fi=fi: e.dma_start(
            xn_fm[:, fi * TQ:(fi + 1) * TQ],
            xn_tm[:, fi * 128:(fi + 1) * 128], transpose=True),
            after=sn_last, sem="xfm")
    t_xfm = ("xfm", prog.cnt["xfm"])
    t_sendxn_last = None
    for fi in range(NF):
        t_sendxn_last = P("sync", lambda e, fi=fi: e.dma_start(
            send_xn[fi * 128:(fi + 1) * 128, :],
            xn_fm[:, fi * TQ:(fi + 1) * TQ]),
            after=[t_xfm], sem="sxn")
    t_ccx = P("gpsimd", lambda e: e.collective_compute(
        "AllGather", ALU.bypass, replica_groups=group4,
        ins=[send_xn.opt()], outs=[gxn.opt()]),
        after=[t_sendxn_last], sem="ccx")

    MB = 4
    NMB = NOFF // MB
    md_i32 = s0b("md_i32", [128, MB * 512], I32)
    md_f32 = s0b("md_f32", [128, MB * 512], F32)
    md_sq = s0b("md_sq", [128, MB * 512], F32)
    md_m32 = s0b("md_m32", [128, MB * 512], F32)
    m_tile = s0b("m_tile", [128, 2 * MB * 512], F16)
    pv = {}
    mst_toks = []
    for mb in range(NMB):
        offs = (mb * MB - 3) * 128
        ti = P("gpsimd", lambda e, offs=offs: e.iota(
            md_i32[:, :], [[128, MB], [1, 512]], base=offs,
            channel_multiplier=-1), after=[pv.get("tc")])
        tc = P("vector", lambda e: e.tensor_copy(md_f32[:, :], md_i32[:, :]),
               after=[ti, pv.get("tm")])
        tm = P("vector", lambda e: e.tensor_single_scalar(
            md_f32[:, :], md_f32[:, :], 0.0, ALU.max), after=[tc, pv.get("tsq")])
        tsq = P("scalar", lambda e: e.activation(md_sq[:, :], md_f32[:, :],
                                                 AF.Sqrt),
                after=[tm, pv.get("tn")] + tok_consts)
        tn = P("vector", lambda e: e.tensor_single_scalar(
            md_m32[:, :], md_sq[:, :], -1.0, ALU.mult),
            after=[tsq, pv.get("tas")])
        tas = P("gpsimd", lambda e, offs=offs: e.affine_select(
            md_m32[:, :], md_m32[:, :], [[128, MB], [1, 512]], ALU.is_ge,
            -MASKBIG, base=offs, channel_multiplier=-1),
            after=[tn, pv.get("t16")])
        t16 = P("vector", lambda e, mb=mb: e.tensor_copy(
            m_tile[:, (mb % 2) * MB * 512:(mb % 2 + 1) * MB * 512],
            md_m32[:, :]),
            after=[tas, mst_toks[mb - 2] if mb >= 2 else None])
        pv = {"tc": tc, "tm": tm, "tsq": tsq, "tn": tn, "tas": tas, "t16": t16}
        mst_toks.append(P("sync", lambda e, mb=mb: e.dma_start(
            m_dram[:, mb * MB * 512:(mb + 1) * MB * 512],
            m_tile[:, (mb % 2) * MB * 512:(mb % 2 + 1) * MB * 512]),
            after=[t16], sem=f"mst{mb % 2}"))
    t_mst = [("mst0", prog.cnt["mst0"]), ("mst1", prog.cnt["mst1"])]

    s0_toks = [(s, prog.cnt[s]) for s in prog.semnames if s != "ccx"]
    for _e in ("sync", "scalar", "vector", "gpsimd"):
        prog.wait(_e, s0_toks)
    s0.close()

    attc = ExitStack()
    ab = lambda n, s, d: attc.enter_context(nc.sbuf_tensor(n, s, d))
    q_sb = ab("q_sb", [128, 4 * S], BF16)
    k_fm = ab("k_fm", [128, S], BF16)
    v_tm = ab("v_tm", [128, S], BF16)
    p_sb = ab("p_sb", [128, 4 * 1024], BF16)
    szpart = ab("szpart", [128, 2 * 512], F32)
    sz16 = ab("sz16", [128, 2 * 512], BF16)
    zrec32 = ab("zrec32", [1, 2 * 512], F32)
    zrecb = ab("zrecb", [1, 2 * 512], BF16)
    o_out = ab("o_out", [128, 2 * 512], BF16)
    o_un = ab("o_un", [128, 4 * 512], F32)

    s1 = ExitStack()
    s1b = lambda n, s, d: s1.enter_context(nc.sbuf_tensor(n, s, d))
    dec_nib = s1b("dec_nib", [128, DNI], U16)
    dec_t7 = s1b("dec_t7", [128, DNI], U16)
    dec_t8 = s1b("dec_t8", [128, DNI], U16)
    dec_fx = s1b("dec_fx", [128, DNI], BF16)
    dec_vals = s1b("dec_vals", [128, 2 * 2816], BF16)

    s2 = ExitStack()
    s2b = lambda n, s, d: s2.enter_context(nc.sbuf_tensor(n, s, d))
    wkvT = s2b("wkvT", [128, NF * 1024], BF16)
    wqT = s2b("wqT", [128, NF * 512], BF16)
    kv_sb = s2b("kv_sb", [128, 2 * 512], BF16)
    xg_ping = s2b("xg_ping", [128, 2 * 2048], BF16)

    t_wkc = P("gpsimd", lambda e: e.dma_start(
        bass.AP(wkvT, 0, [[NF * 1024, 128], [1024, NF], [1, 512]]),
        bass.AP(wk_in.tensor, 0, [[512, 128], [128 * 512, NF], [1, 512]])),
        sem="wkc")
    t_wvc = P("gpsimd", lambda e: e.dma_start(
        bass.AP(wkvT, 512, [[NF * 1024, 128], [1024, NF], [1, 512]]),
        bass.AP(wv_in.tensor, 0, [[512, 128], [128 * 512, NF], [1, 512]])),
        sem="wvc")
    t_wqc = P("gpsimd", lambda e: e.dma_start(
        bass.AP(wqT, 0, [[NF * 512, 128], [512, NF], [1, 512]]),
        bass.AP(wq_in.tensor, 0, [[512, 128], [128 * 512, NF], [1, 512]])),
        sem="wqc")
    t_wodr = P("gpsimd", lambda e: e.dma_start(
        bass.AP(w_dram.tensor, 0, [[DM, DM], [1, DM]]),
        bass.AP(wo_in.tensor, 0, [[DM, DM], [1, DM]])), sem="woc")

    t_gusf = P("vector", lambda e: e.tensor_copy(gusf[:, :], gus16[:, :]),
               after=[t_ld1])
    t_dnsf = P("vector", lambda e: e.tensor_copy(dnsf[:, :], dns16[:, :]),
               after=[t_ld1])

    def decode_tile(handle, row_len, col_off, n_u16, scale_fn, pos, cpar,
                    t_after, eng="vector"):
        nnib = 4 * n_u16
        vo = cpar * 2816 + pos * nnib
        rep = bass.AP(handle, col_off, [[row_len, 128], [1, n_u16], [0, 4]])
        nib3 = bass.AP(dec_nib, 0, [[DNI, 128], [4, n_u16], [1, 4]])
        amt3 = bass.AP(amt16, 0, [[DNI, 128], [4, n_u16], [1, 4]])
        t1 = P(eng, lambda e: e.tensor_tensor(
            nib3, rep, amt3, ALU.logical_shift_right), after=t_after)
        t3 = P(eng, lambda e: e.tensor_scalar(
            dec_t7[:, :nnib], dec_nib[:, :nnib], 7, 6,
            ALU.bitwise_and, ALU.logical_shift_left), after=[t1])
        t5 = P(eng, lambda e: e.tensor_scalar(
            dec_t8[:, :nnib], dec_nib[:, :nnib], 8, 12,
            ALU.bitwise_and, ALU.logical_shift_left), after=[t1])
        t6 = P(eng, lambda e: e.tensor_single_scalar(
            dec_t7[:, :nnib], dec_t7[:, :nnib], 0x3F00, ALU.add), after=[t3])
        bview = bass.AP(dec_t7, 0, [[DNI, 128], [1, nnib]]).bitcast(BF16)
        t7_ = P(eng, lambda e: e.tensor_scalar(
            dec_fx[:, :nnib], bview, 1.0, 0.0, ALU.subtract, ALU.min),
            after=[t6])
        vpos = bass.AP(dec_vals, vo, [[2 * 2816, 128], [1, nnib]]).bitcast(BF16)
        t8_ = P(eng, lambda e: e.tensor_tensor(
            vpos, bview, dec_fx[:, :nnib], ALU.add), after=[t7_])
        valsu = bass.AP(dec_vals, vo, [[2 * 2816, 128], [1, nnib]]).bitcast(U16)
        t9 = P(eng, lambda e: e.tensor_tensor(
            valsu, valsu, dec_t8[:, :nnib], ALU.add), after=[t8_, t5])
        return scale_fn(t9, vo)

    dq_toks = []
    dq_store_fns = []
    prev_dve = None
    for chunk in range(8):
        chunk_dves = []
        for k in range(11):
            it = chunk * 11 + k

            def mk_scale(it=it):
                def scale_fn(tok, vo):
                    a = P("vector", lambda e, it=it, vo=vo: e.tensor_scalar(
                        dec_vals[:, vo:vo + 128], dec_vals[:, vo:vo + 128],
                        gusf[:, 2 * it:2 * it + 1], None, ALU.mult),
                        after=[tok, t_gusf])
                    return P("vector", lambda e, it=it, vo=vo: e.tensor_scalar(
                        dec_vals[:, vo + 128:vo + 256],
                        dec_vals[:, vo + 128:vo + 256],
                        gusf[:, 2 * it + 1:2 * it + 2], None, ALU.mult),
                        after=[a])
                return scale_fn
            prev_dve = decode_tile(
                sw_gu, NIT * GUC2, it * GUC2, GUC2, mk_scale(), k, chunk % 2,
                [t_ld1, t_amt, prev_dve,
                 dq_toks[chunk - 2] if chunk >= 2 and k == 0 else None])
            chunk_dves.append(prev_dve)
        ci = chunk
        tok = ("dq%d" % ci, 16)
        prog.cnt["dq%d" % ci] = 0

        def store_fn(e, ci=ci):
            return e.dma_start(
                bass.AP(w_dec_gu.tensor, ci * 11 * 128 * 256,
                        [[256, 128], [128 * 256, 11], [1, 256]]),
                bass.AP(dec_vals, (ci % 2) * 2816,
                        [[2 * 2816, 128], [256, 11], [1, 256]]))
        dq_store_fns.append(("dq%d" % ci, store_fn, list(chunk_dves)))
        dq_toks.append(tok)

    t_dgu_dve_last = prev_dve
    for semname, fn, deps in dq_store_fns[:6]:
        P("gpsimd", fn, after=deps, sem=semname)
    t_dgu_done = dq_toks[:8]

    for chunk in range(4):
        chunk_dves = []
        for k in range(4):
            ot = chunk * 4 + k

            def mk_scale_dn(ot=ot):
                def scale_fn(tok, vo):
                    last = tok
                    for s in range(11):
                        last = P("vector", lambda e, s=s, ot=ot, vo=vo:
                                 e.tensor_scalar(
                            dec_vals[:, vo + s * 64:vo + (s + 1) * 64],
                            dec_vals[:, vo + s * 64:vo + (s + 1) * 64],
                            dnsf[:, 11 * ot + s:11 * ot + s + 1], None,
                            ALU.mult), after=[last, t_dnsf])
                    return last
                return scale_fn
            prev_dve = decode_tile(
                sw_dn, NOT_ * DNC2, ot * DNC2, DNC2, mk_scale_dn(), k,
                chunk % 2,
                [t_ld1, t_amt, prev_dve,
                 (dq_toks[8 + chunk - 2] if chunk >= 2 else dq_toks[6 + chunk])
                 if k == 0 else None])
            chunk_dves.append(prev_dve)
        ci = 8 + chunk

        def store_fn(e, chunk=chunk):
            return e.dma_start(
                bass.AP(w_dec_dn.tensor, chunk * 4 * 128 * DNI,
                        [[DNI, 128], [128 * DNI, 4], [1, DNI]]),
                bass.AP(dec_vals, (chunk % 2) * 2816,
                        [[2 * 2816, 128], [704, 4], [1, 704]]))
        dq_store_fns.append(("dq%d" % ci, store_fn, list(chunk_dves)))
        dq_toks.append(("dq%d" % ci, 16))

    t_ddn_dve_last = prev_dve
    t_ddn_done = dq_toks[8:]





    kv_free = [None, None]
    kv_sent = [None, None]
    t_kv_send_last = None
    for tb in range(NQB):
        for c8 in range(8):
            bank = (tb * 8 + c8) % 2
            off0 = c8 * 128 if c8 < 4 else 512 + (c8 - 4) * 128
            last = None
            for fi in range(NF):
                a = [t_wkc if c8 < 4 else t_wvc, t_xfm]
                if fi == 0:
                    a.append(kv_free[bank])
                last = P("tensor", lambda e, fi=fi, off0=off0, tb=tb, bank=bank:
                         e.matmul(
                    psA[bank],
                    wkvT[:, fi * 1024 + off0: fi * 1024 + off0 + 128],
                    xn_fm[:, fi * TQ + tb * 512: fi * TQ + (tb + 1) * 512],
                    start=(fi == 0), stop=(fi == NF - 1)),
                    after=a, sem="pe" if fi == NF - 1 else None)
            ev = P("scalar", lambda e, bank=bank: e.activation(
                kv_sb[:, bank * 512:(bank + 1) * 512],
                psA[bank], AF.Copy, bias=0.0, scale=1.0),
                after=[last, kv_sent[bank]] + tok_consts)
            kv_free[bank] = ev
            t_kv_pe_last = last
            row = 256 * (c8 % 4) + (0 if c8 < 4 else 128)
            kv_sent[bank] = P("sync", lambda e, tb=tb, row=row, bank=bank:
                              e.dma_start(
                send_kv[row:row + 128, tb * 512:(tb + 1) * 512],
                kv_sb[:, bank * 512:(bank + 1) * 512]),
                after=[ev], sem=f"skv{bank}")

    t_cck = P("gpsimd", lambda e: e.collective_compute(
        "AllGather", ALU.bypass, replica_groups=group4,
        ins=[send_kv.opt()], outs=[g_kv.opt()]),
        after=[kv_sent[0], kv_sent[1]], sem="cck")
    for semname, fn, deps in dq_store_fns[6:]:
        P("gpsimd", fn, after=deps, sem=semname)

    q_free = [None] * 4
    q_last = [None] * 4
    xg_free = [None, None]
    t_xg_last = [None, None]
    for tb in range(S // 512):
        tq, half = tb // NQB, tb % NQB
        for fq in range(4):
            pp = (tb * 4 + fq) % 2
            txg = P("sync", lambda e, tq=tq, half=half, fq=fq, pp=pp:
                    e.dma_start(
                bass.AP(xg_ping, pp * 2048,
                        [[2 * 2048, 128], [512, 4], [1, 512]]),
                bass.AP(gxn.tensor,
                        (DM * tq + fq * 512) * TQ + half * 512,
                        [[TQ, 128], [128 * TQ, 4], [1, 512]])),
                after=[t_ccx, xg_free[pp]], sem=f"xg{pp}")
            t_xg_last[pp] = txg
            for fj in range(4):
                fi = fq * 4 + fj
                for hl in range(4):
                    need_tok = (fi == NF - 1) or (hl == 3 and fj == 3)
                    a = [txg, t_wqc]
                    if fi == 0:
                        a.append(q_free[hl])
                    tok = P("tensor", lambda e, fi=fi, fj=fj, hl=hl, pp=pp:
                            e.matmul(
                        psA[2 + hl],
                        wqT[:, fi * 512 + hl * 128: fi * 512 + (hl + 1) * 128],
                        xg_ping[:, pp * 2048 + fj * 512: pp * 2048 + (fj + 1) * 512],
                        start=(fi == 0), stop=(fi == NF - 1)),
                        after=a, sem="pe" if need_tok else None)
                    if hl == 3 and fj == 3:
                        xg_free[pp] = tok
                    if need_tok:
                        t_q_pe_last = tok
                    if fi == NF - 1:
                        q_free[hl] = P("scalar", lambda e, hl=hl, tb=tb:
                                       e.activation(
                            q_sb[:, hl * S + tb * 512: hl * S + (tb + 1) * 512],
                            psA[2 + hl], AF.Copy, bias=0.0,
                            scale=cp[:, hl:hl + 1]),
                            after=[tok, t_ld0] + tok_consts)
                        q_last[hl] = q_free[hl]

    def kv_slice(e):
        with e.register("rkv") as rkv:
            e.reg_load(rkv, meta_sb[0:1, 0:1])
            return e.dma_start(
                recv_kv,
                bass.AP(g_kv.tensor, rkv, [[1024 * TQ, 4], [TQ, 256], [1, TQ]]))
    t_gkv = P("sync", kv_slice, after=[t_cck, t_ld0], sem="gkv")

    t_kfm_last = None
    for tq in range(4):
        t_kfm_last = P("sync", lambda e, tq=tq: e.dma_start(
            k_fm[:, tq * TQ:(tq + 1) * TQ], recv_kv[256 * tq:256 * tq + 128, :]),
            after=[t_gkv], sem="kfm")

    t_vtm_last = None
    for tq in range(4):
        for tl in range(TQ // 128):
            j = tq * (TQ // 128) + tl
            t_vtm_last = P("sync", lambda e, tq=tq, tl=tl, j=j: e.dma_start(
                v_tm[:, j * 128:(j + 1) * 128],
                recv_kv[256 * tq + 128:256 * tq + 256, tl * 128:(tl + 1) * 128],
                transpose=True), after=[t_gkv], sem="vtm")

    s2.close()
    attc2 = ExitStack()
    ab2 = lambda n, s, d: attc2.enter_context(nc.sbuf_tensor(n, s, d))
    m_sb = ab2("m_sb", [128, NOFF * 512], F16)
    wt2 = ab2("wt2", [128, II], BF16)
    reuse_toks = [t_q_pe_last, t_kv_pe_last, kv_sent[0], kv_sent[1],
                  t_xg_last[0], t_xg_last[1], t_wkc, t_wvc, t_wqc,
                  q_free[0], q_free[1], q_free[2], q_free[3]]
    t_m = []
    NMC = (NOFF + 7) // 8
    for mc in range(NMC):
        tl = P("sync", lambda e, mc=mc: e.dma_start(
            m_sb[:, mc * 4096:(mc + 1) * 4096],
            m_dram[:, mc * 4096:(mc + 1) * 4096]),
            after=reuse_toks + t_mst, sem=f"mld{mc}")
        t_m += [tl] * 8
    t_m = t_m[:NOFF]

    swg_fns = []
    t_wts_prev = None
    sgi = 0
    for fc in range(2):
        for half in range(2):
            t_wts_prev = P("sync", lambda e, fc=fc, half=half: e.dma_start(
                wt2[:, :],
                w_dec_gu[half * II:(half + 1) * II, fc * 128:(fc + 1) * 128],
                transpose=True),
                after=t_dgu_done + reuse_toks +
                [("sg%d" % (sgi - 1), 16) if sgi else None],
                sem="wts")
            swg_fns.append(("sg%d" % sgi, fc, half, t_wts_prev))
            sgi += 1
    t_swg = []
    for semname, fc, half, dep in swg_fns:
        t_swg.append(P("gpsimd", lambda e, fc=fc, half=half: e.dma_start(
            bass.AP(send_wgu.tensor, fc * 128 + half * 44 * 256,
                    [[22528, 128], [256, 44], [1, 128]]),
            bass.AP(wt2, 0, [[II, 128], [128, 44], [1, 128]])),
            after=[dep], sem=semname))
    t_ccg = P("gpsimd", lambda e: e.collective_compute(
        "AllGather", ALU.bypass, replica_groups=group8,
        ins=[send_wgu.opt()], outs=[g_wgu.opt()]), after=t_swg, sem="ccg")

    starts = list(range(0, DNI - 127, 128))
    if starts[-1] + 128 < DNI:
        starts.append(DNI - 128)
    prev_end = 0
    dn_sends = []
    for p0 in starts:
        skip = max(0, prev_end - p0)
        prev_end = p0 + 128
        t_wts_prev = P("sync", lambda e, p0=p0: e.dma_start(
            wt2[0:128, 0:DM], w_dec_dn[:, p0:p0 + 128], transpose=True),
            after=t_ddn_done + [("sg%d" % (sgi - 1), 16)], sem="wts")
        dn_sends.append((("sg%d" % sgi), p0, skip, t_wts_prev))
        sgi += 1
    t_swd = []
    for semname, p0, skip, dep in dn_sends:
        t_swd.append(P("gpsimd", lambda e, p0=p0, skip=skip: e.dma_start(
            send_wdn[p0 + skip:p0 + 128, :], wt2[skip:128, 0:DM]),
            after=[dep], sem=semname))
    t_ccd = P("gpsimd", lambda e: e.collective_compute(
        "AllGather", ALU.bypass, replica_groups=group8,
        ins=[send_wdn.opt()], outs=[g_wdn.opt()]), after=t_swd, sem="ccd")



    blocks = [(hl, qi) for hl in range(4) for qi in range(NQT)]
    st = []
    for p in range(2):
        st.append({
            "pss": psP[p],
            "pso": psA[4 + p],
            "po": p * 2048,
            "zo": p * 512,
            "oo": p * 512,
            "s_free": [kv_free[0], kv_free[1]] if p == 0
                      else [q_free[0], q_free[1]],
            "o_free": q_free[2 + p],
            "t_z": None,
            "t_av": None,
            "so_last": None,
            "p_free": [None, None],
            "oun_free": [None, None],
            "nfin": 0,
        })
    pszS = psA[6]
    pszR = psA[7]
    zf = {}

    def emit_score(p, hl, qi, pr):
        s = st[p]
        sl = pr % 2
        s["slot"] = sl
        po = s["po"] + sl * 1024
        for u in range(2):
            k = 2 * pr + u
            moff = 4 * qi - k + 3
            P("tensor", lambda e, s=s, u=u, moff=moff: e.matmul(
                s["pss"][:, u * 512:(u + 1) * 512], id16[:, :],
                m_sb[:, moff * 512:(moff + 1) * 512],
                start=True, stop=False),
                after=[t_m[moff], s["s_free"], t_id], sem=None)
            tm_ = P("tensor", lambda e, s=s, u=u, k=k, hl=hl, qi=qi: e.matmul(
                s["pss"][:, u * 512:(u + 1) * 512],
                k_fm[:, k * 128:(k + 1) * 128],
                q_sb[:, hl * S + qi * 512:hl * S + (qi + 1) * 512],
                start=False, stop=True),
                after=[t_kfm_last, q_last[hl]],
                sem="pe" if u == 1 else None)
        t_p = P("scalar", lambda e, s=s, hl=hl, po=po: e.activation(
            p_sb[:, po:po + 1024], s["pss"][:, :],
            AF.Exp, bias=-float(SHIFT), scale=cp[:, 4 + hl:5 + hl]),
            after=[tm_, s["p_free"][sl], t_ld0] + tok_consts)
        s["s_free"] = t_p
        if pr == 0:
            s["t_z"] = P("vector", lambda e, s=s, po=po: e.tensor_tensor(
                szpart[:, s["zo"]:s["zo"] + 512],
                p_sb[:, po:po + 512],
                p_sb[:, po + 512:po + 1024], ALU.add),
                after=[t_p, s.get("tz16_prev")])
        else:
            tza = P("vector", lambda e, s=s, po=po: e.tensor_tensor(
                szpart[:, s["zo"]:s["zo"] + 512],
                szpart[:, s["zo"]:s["zo"] + 512],
                p_sb[:, po:po + 512], ALU.add),
                after=[t_p, s["t_z"]])
            s["t_z"] = P("vector", lambda e, s=s, po=po: e.tensor_tensor(
                szpart[:, s["zo"]:s["zo"] + 512],
                szpart[:, s["zo"]:s["zo"] + 512],
                p_sb[:, po + 512:po + 1024], ALU.add),
                after=[tza])
        s["t_p"] = t_p

    def emit_av(p, hl, qi, pr, nkt):
        s = st[p]
        t_p = s["t_p"]
        sl = s["slot"]
        po = s["po"] + sl * 1024
        for u in range(2):
            k = 2 * pr + u
            s["t_av"] = P("tensor", lambda e, s=s, u=u, k=k, nkt=nkt, po=po:
                          e.matmul(
                s["pso"], v_tm[:, k * 128:(k + 1) * 128],
                p_sb[:, po + u * 512:po + (u + 1) * 512],
                start=(k == 0), stop=(k == nkt - 1)),
                after=[t_p, t_vtm_last, s["o_free"] if k == 0 else None],
                sem="pe" if u == 1 else None)
        s["p_free"][sl] = [s["t_av"], s["t_z"]]

    def emit_fin1(p, hl, qi):
        s = st[p]
        pa = s["nfin"] % 2
        oev = P("scalar", lambda e, s=s, pa=pa: e.activation(
            o_un[:, (2 * p + pa) * 512:(2 * p + pa + 1) * 512], s["pso"],
            AF.Copy, bias=0.0, scale=1.0),
            after=[s["t_av"], s["oun_free"][pa]] + tok_consts)
        s["o_free"] = oev
        s["oev"] = oev
        tz16 = P("vector", lambda e, s=s: e.tensor_copy(
            sz16[:, s["zo"]:s["zo"] + 512], szpart[:, s["zo"]:s["zo"] + 512]),
            after=[s["t_z"], s.get("tzs_prev")])
        s["tz16_prev"] = tz16

    def emit_fin2a(p, hl, qi):
        s = st[p]
        tzs = P("tensor", lambda e, s=s: e.matmul(
            pszS[0:1, :], ones_col[:, :], sz16[:, s["zo"]:s["zo"] + 512],
            start=True, stop=True),
            after=[s["tz16_prev"], t_on1, zf.get("tzr0")])
        s["tzs_prev"] = tzs
        tzr0 = P("vector", lambda e, s=s: e.reciprocal(
            zrec32[0:1, s["zo"]:s["zo"] + 512], pszS[0:1, :]),
            after=[tzs, s.get("tzr_prev")])
        zf["tzr0"] = tzr0
        s["tzr_prev"] = P("vector", lambda e, s=s: e.tensor_copy(
            zrecb[0:1, s["zo"]:s["zo"] + 512],
            zrec32[0:1, s["zo"]:s["zo"] + 512]), after=[tzr0])

    def emit_fin2b(p, hl, qi):
        s = st[p]
        pa = s["nfin"] % 2
        s["nfin"] += 1
        tzrep = P("tensor", lambda e, s=s: e.matmul(
            pszR, ones_row[:, :], zrecb[0:1, s["zo"]:s["zo"] + 512],
            start=True, stop=True),
            after=[s["tzr_prev"], t_on2, zf.get("tev")])
        tev = P("vector", lambda e, s=s, pa=pa: e.tensor_tensor(
            o_out[:, s["oo"]:s["oo"] + 512],
            o_un[:, (2 * p + pa) * 512:(2 * p + pa + 1) * 512],
            pszR, ALU.mult),
            after=[tzrep, s["oev"], s["so_last"]])
        zf["tev"] = tev
        s["oun_free"][pa] = tev
        row = 128 * hl
        col = qi * 512
        s["so_last"] = P("scalar", lambda e, s=s, row=row, col=col: e.dma_start(
            send_o[row:row + 128, col:col + 512],
            o_out[:, s["oo"]:s["oo"] + 512]), after=[tev], sem=f"so{p}")

    streams = []
    for p in range(2):
        seq = []
        myblocks = [blk for i, blk in enumerate(blocks)
                    if (i // NQT) % 2 == p]
        pend = []
        for (hl, qi) in myblocks:
            nkt = 4 * qi + 4
            items = []
            for pr in range(nkt // 2):
                items.append(("score", hl, qi, pr, nkt))
                items.append(("av", hl, qi, pr, nkt))
            out = [items[0]]
            if pend:
                out.append(pend.pop(0))
            out.append(items[1])
            if pend:
                out.append(pend.pop(0))
            out += items[2:]
            seq += out
            seq.append(("fin1", hl, qi, 0, 0))
            pend = [("fin2a", hl, qi, 0, 0), ("fin2b", hl, qi, 0, 0)]
        seq += pend
        streams.append(seq)
    ii = [0, 0]
    while ii[0] < len(streams[0]) or ii[1] < len(streams[1]):
        for p in range(2):
            if ii[p] < len(streams[p]):
                kind, hl, qi, pr, nkt = streams[p][ii[p]]
                ii[p] += 1
                if kind == "score":
                    emit_score(p, hl, qi, pr)
                elif kind == "av":
                    emit_av(p, hl, qi, pr, nkt)
                elif kind == "fin1":
                    emit_fin1(p, hl, qi)
                elif kind == "fin2a":
                    emit_fin2a(p, hl, qi)
                else:
                    emit_fin2b(p, hl, qi)
    so_last = [st[0]["so_last"], st[1]["so_last"]]

    t_cco = P("gpsimd", lambda e: e.collective_compute(
        "AllGather", ALU.bypass, replica_groups=group4,
        ins=[send_o.opt()], outs=[g_o.opt()]), after=so_last, sem="cco")

    def o_slice(e):
        with e.register("ro") as ro:
            e.reg_load(ro, meta_sb[0:1, 1:2])
            return e.dma_start(
                recv_o, bass.AP(g_o.tensor, ro, [[S, 2048], [1, TQ]]))
    t_go = P("gpsimd", o_slice, after=[t_cco, t_ld0], sem="go")

    att_toks = [(s, prog.cnt[s]) for s in prog.semnames
                if s not in ("cco", "go", "woc", "ccg", "ccd")]

    for _e in ENGINES:
        prog.wait(_e, att_toks)
    attc2.close()
    s1.close()
    attc.close()

    gctx = ExitStack()
    g_sb = gctx.enter_context(nc.sbuf_tensor("g_sb", [128, NTT * DM], F16))
    fctx = ExitStack()
    fsb = lambda n, s, d: fctx.enter_context(nc.sbuf_tensor(n, s, d))
    wblk = [fsb("wblk0", [128, 2 * 4096], BF16),
            fsb("wblk1", [128, 2 * 4096], BF16)]
    h_sb = fsb("h_sb", [128, 44 * 512], BF16)
    silu_t = fsb("silu_t", [128, 512], F32)
    y_sb = fsb("y_sb", [128, 4 * DM], F32)
    dnstr = fsb("dnstr", [128, 2 * DM], BF16)
    ydump = fsb("ydump", [128, DM], BF16)
    rstat = fsb("rstat", [128, 2], F32)

    NHALF = TQ // 512
    NTH = 4
    t_g_done = {}
    bank_free = [None] * 4
    wgu_free = [None, None]
    dn_slot_free = [None] * 8
    pass_bank_free = [None] * 4
    h_tile_free = {}
    y_tile_free = {}
    rs_chain = {}
    sdctr = [0]
    t_down_last = [None]
    h_last = [None]
    for h2 in range(NHALF):
        h_toks = []
        yev = {}

        def gate_jb(jb, h2=h2, h_toks=None):
            ppw = jb % 2
            wb = wblk[ppw]
            j0 = 2 * jb
            tgb = P("sync", lambda e, j0=j0, wb=wb: e.dma_start(
                bass.AP(wb, 0, [[2 * 4096, 128], [512, 8], [1, 512]]),
                bass.AP(g_wgu.tensor, j0 * 256,
                        [[22528, 128], [2883584, 8], [1, 512]])),
                after=[t_ccg, wgu_free[ppw]], sem=f"wb{ppw}")
            tgb = P("sync", lambda e, j0=j0, wb=wb: e.dma_start(
                bass.AP(wb, 4096, [[2 * 4096, 128], [512, 8], [1, 512]]),
                bass.AP(g_wgu.tensor, (44 + j0) * 256,
                        [[22528, 128], [2883584, 8], [1, 512]])),
                after=[t_ccg, wgu_free[ppw]], sem=f"wb{ppw}")
            for jj in range(2):
                j = j0 + jj
                bk = jj * 2
                lastA = lastB = None
                for fi in range(NF):
                    r, flo = fi // 2, fi % 2
                    woff = r * 512 + jj * 256 + flo * 128
                    aA = [tgb, t_xfm]
                    if fi == 0:
                        aA.append(bank_free[bk])
                    lastA = P("tensor", lambda e, fi=fi, wb=wb, bk=bk, h2=h2,
                              woff=woff: e.matmul(
                        psA[bk], wb[:, woff:woff + 128],
                        xn_fm[:, fi * TQ + h2 * 512: fi * TQ + h2 * 512 + 512],
                        start=(fi == 0), stop=(fi == NF - 1)),
                        after=aA, sem="pe" if fi == NF - 1 else None)
                for fi in range(NF):
                    r, flo = fi // 2, fi % 2
                    woff = 4096 + r * 512 + jj * 256 + flo * 128
                    aB = [tgb, t_xfm]
                    if fi == 0:
                        aB.append(bank_free[bk + 1])
                    lastB = P("tensor", lambda e, fi=fi, wb=wb, bk=bk, h2=h2,
                              woff=woff: e.matmul(
                        psA[bk + 1], wb[:, woff:woff + 128],
                        xn_fm[:, fi * TQ + h2 * 512: fi * TQ + h2 * 512 + 512],
                        start=(fi == 0), stop=(fi == NF - 1)),
                        after=aB, sem="pe" if fi == NF - 1 else None)
                if jj == 1:
                    wgu_free[ppw] = lastB
                tsi = P("scalar", lambda e, bk=bk: e.activation(
                    silu_t[:, :], psA[bk], AF.Silu),
                    after=[lastA, h_toks[-1] if h_toks else None] + tok_consts)
                bank_free[bk] = tsi
                th = P("vector", lambda e, j=j, bk=bk: e.tensor_tensor(
                    h_sb[:, j * 512:(j + 1) * 512], silu_t[:, :],
                    psA[bk + 1], ALU.mult),
                    after=[tsi, lastB, h_tile_free.get(j)] +
                    (dq_toks[8:] if j == 8 and h2 == 0 else []))
                bank_free[bk + 1] = th
                h_last[0] = th
                h_toks.append(th)

        def down_step(ps, j, h2=h2, h_toks=None, yev=None):
            ch, cc = ps // 2, ps % 2
            slot = sdctr[0] % 8
            sdctr[0] += 1
            tds = P("sync" if slot % 2 == 0 else "scalar",
                    lambda e, j=j, ch=ch, cc=cc, slot=slot: e.dma_start(
                dnstr[:, slot * 512:(slot + 1) * 512],
                g_wdn[j * 128:(j + 1) * 128,
                      ch * 1024 + cc * 512:ch * 1024 + (cc + 1) * 512]),
                after=[t_ccd, dn_slot_free[slot]], sem=f"dn{slot}")
            lastm = [None] * 4
            for bt in range(4):
                a = [tds, h_toks[j]]
                if j == 0:
                    a.append(pass_bank_free[bt])
                lastm[bt] = P("tensor", lambda e, j=j, bt=bt, slot=slot:
                              e.matmul(
                    psA[4 + bt],
                    h_sb[:, j * 512 + bt * 128: j * 512 + (bt + 1) * 128],
                    dnstr[:, slot * 512:(slot + 1) * 512],
                    start=(j == 0), stop=(j == 43)),
                    after=a, sem="pe" if (bt == 3 or j == 43) else None)
            dn_slot_free[slot] = lastm[3]
            t_down_last[0] = lastm[3]
            if ps == 3:
                h_tile_free[j] = lastm[3]
            if j == 43:
                for bt in range(4):
                    tev = P("vector", lambda e, bt=bt, ch=ch, cc=cc:
                            e.tensor_copy(
                        y_sb[:, bt * DM + ch * 1024 + cc * 512:
                             bt * DM + ch * 1024 + (cc + 1) * 512],
                        psA[4 + bt]),
                        after=[lastm[bt], y_tile_free.get(bt)] +
                        (t_swd if h2 == 0 and ps == 0 and bt == 0 else []))
                    pass_bank_free[bt] = tev
                    yev[(bt, ps)] = tev

        down_sched = [(ps, j) for ps in range(4) for j in range(44)]
        di = 0
        for jb in range(22):
            gate_jb(jb, h2, h_toks)
            navail = 2 * (jb + 1)
            while di < len(down_sched) and di < (jb + 1) * 8:
                ps, j = down_sched[di]
                if ps == 0 and j >= navail:
                    break
                down_step(ps, j, h2, h_toks, yev)
                di += 1
        while di < len(down_sched):
            ps, j = down_sched[di]
            down_step(ps, j, h2, h_toks, yev)
            di += 1

        for tt in range(NTH):
            tglob = h2 * NTH + tt
            tsq = P("scalar", lambda e, tt=tt: e.activation(
                ydump[:, :], y_sb[:, tt * DM:(tt + 1) * DM], AF.Square,
                accum_out=rstat[:, 0:1]),
                after=[yev[(tt, ps)] for ps in range(4)] +
                [rs_chain.get("tsr"), rs_chain.get("tyn"),
                 rs_chain.get("tg")] + tok_consts)
            tsr = P("scalar", lambda e: e.activation(
                rstat[:, 1:2], rstat[:, 0:1], AF.Sqrt, bias=1e-6,
                scale=1.0 / DM), after=[tsq, rs_chain.get("trc")])
            trc = P("vector", lambda e: e.reciprocal(rstat[:, 1:2], rstat[:, 1:2]),
                    after=[tsr])
            tyn = P("vector", lambda e, tt=tt: e.tensor_scalar(
                y_sb[:, tt * DM:(tt + 1) * DM],
                y_sb[:, tt * DM:(tt + 1) * DM], rstat[:, 1:2],
                None, ALU.mult), after=[trc, tsq, rs_chain.get("tg")])
            tg = P("scalar", lambda e, tt=tt, tglob=tglob: e.activation(
                g_sb[:, tglob * DM:(tglob + 1) * DM],
                y_sb[:, tt * DM:(tt + 1) * DM], AF.Silu),
                after=[tyn])
            y_tile_free[tt] = tyn
            rs_chain = {"tsr": tsr, "trc": trc, "tyn": tyn, "tg": tg}
            t_g_done[tglob] = tg

    pse = [psA[0], psA[1]]

    def alias(name, shape, dtype, target, off=0):
        return nc.alloc_sbuf_tensor_at(
            name, shape, dtype, offset=nc.lookup_mloc(target).addr + off)

    woq = [alias("woq0", [128, 4 * DM], BF16, wblk[0]),
           alias("woq1", [128, 4 * DM], BF16, wblk[1]),
           alias("woq2", [128, 4 * DM], BF16, h_sb),
           alias("woq3", [128, 4 * DM], BF16, h_sb, 4 * DM * 2)]
    ostrip_p = alias("ostrip_p", [128, 2 * 2048], BF16, sw_gu)
    xep = alias("xep", [128, 2 * 512], F32, sw_gu, 8192)
    etmp = alias("etmp", [128, 2 * 512], F32, sw_gu, 12288)

    wb_tot = [("wb0", prog.cnt["wb0"]), ("wb1", prog.cnt["wb1"])]
    woq_tok = []
    for i in range(4):
        gate_i = ([wgu_free[0], wgu_free[1], t_wodr] + wb_tot) if i < 2 else                  ([t_down_last[0], h_last[0], t_wodr])
        woq_tok.append(P("sync" if i % 2 == 0 else "scalar",
                         lambda e, i=i: e.dma_start(
            bass.AP(woq[i], 0, [[4 * DM, 128], [DM, 4], [1, DM]]),
            bass.AP(w_dram.tensor, i * 4 * 128 * DM,
                    [[DM, 128], [128 * DM, 4], [1, DM]])),
            after=gate_i, sem=f"wq{i}"))

    t_out_last = [None, None]
    e_ps_free = [None, None]
    etmp_free = [None, None]
    xep_free = [None, None]
    ost_free = [None, None]
    smm_bank0 = [bank_free[0], bank_free[1]]
    dec_guard = [t_dgu_dve_last, t_ddn_dve_last] + dq_toks
    for tt in range(NTT):
        pp = tt % 2
        tos = P("sync" if pp == 0 else "scalar", lambda e, tt=tt, pp=pp:
                e.dma_start(
            bass.AP(ostrip_p, pp * 2048, [[2 * 2048, 128], [128, 16], [1, 128]]),
            bass.AP(recv_o.tensor, tt * 128,
                    [[TQ, 128], [128 * TQ, 16], [1, 128]])),
            after=[t_go, ost_free[pp]] + dec_guard, sem=f"os{pp}")
        for cc4 in range(4):
            eb = cc4 % 2
            last = None
            for chi in range(NF):
                a = [tos, woq_tok[chi // 4]]
                if chi == 0:
                    a.append(e_ps_free[eb])
                    a.append(smm_bank0[eb])
                last = P("tensor", lambda e, tt=tt, cc4=cc4, chi=chi, eb=eb,
                         pp=pp: e.matmul(
                    pse[eb],
                    ostrip_p[:, pp * 2048 + chi * 128: pp * 2048 + (chi + 1) * 128],
                    woq[chi // 4][:, (chi % 4) * DM + cc4 * 512:
                                  (chi % 4) * DM + (cc4 + 1) * 512],
                    start=(chi == 0), stop=(chi == NF - 1)),
                    after=a, sem="pe" if chi == NF - 1 else None)
                if cc4 == 3 and chi == NF - 1:
                    ost_free[pp] = last
            telx = P("sync", lambda e, cc4=cc4, tt=tt, eb=eb: e.dma_start(
                xep[:, eb * 512:(eb + 1) * 512],
                x_own[tt * 128:(tt + 1) * 128, cc4 * 512:(cc4 + 1) * 512]),
                after=[xep_free[eb]] + dec_guard, sem=f"xp{eb}")
            te1 = P("vector", lambda e, cc4=cc4, tt=tt, eb=eb: e.tensor_tensor(
                etmp[:, eb * 512:(eb + 1) * 512], pse[eb],
                g_sb[:, tt * DM + cc4 * 512: tt * DM + (cc4 + 1) * 512],
                ALU.mult),
                after=[last, t_g_done.get(tt), etmp_free[eb]] + dec_guard)
            e_ps_free[eb] = te1
            te2 = P("vector", lambda e, eb=eb: e.tensor_tensor(
                etmp[:, eb * 512:(eb + 1) * 512],
                etmp[:, eb * 512:(eb + 1) * 512],
                xep[:, eb * 512:(eb + 1) * 512], ALU.add),
                after=[telx, te1])
            xep_free[eb] = te2
            tst = P("sync", lambda e, cc4=cc4, tt=tt, eb=eb: e.dma_start(
                out_own[tt * 128:(tt + 1) * 128, cc4 * 512:(cc4 + 1) * 512],
                etmp[:, eb * 512:(eb + 1) * 512]), after=[te2], sem=f"ou{eb}")
            etmp_free[eb] = tst
            t_out_last[eb] = tst

    prog.wait("sync", t_out_last)
    prog.emit()
    fctx.close()
    gctx.close()
    sP.close()
    ctx.close()
    return nc


def _in_maps(x, Wq, Wk, Wv, Wo, gupk, gus, dnpk, dns):
    B, S, _ = x.shape
    TQ = S * 2 // 8
    WkT = np.ascontiguousarray(Wk.T)
    WvT = np.ascontiguousarray(Wv.T)
    WoT = np.ascontiguousarray(Wo.T)
    maps = []
    for r in range(8):
        b, g = r // 4, r % 4
        slopes = np.array([2.0 ** (-0.5 * (4 * g + j + 1)) for j in range(4)],
                          np.float64)
        consts = np.zeros((128, 8), np.float32)
        consts[:, 0:4] = (1.0 / (np.sqrt(128.0) * slopes))[None, :].astype(np.float32)
        consts[:, 4:8] = slopes[None, :].astype(np.float32)
        gup_s = np.ascontiguousarray(gupk[:, 32 * r:32 * (r + 1)]).view(np.uint16)
        gup_s = np.ascontiguousarray(
            gup_s.reshape(88, 128, 64).transpose(1, 0, 2).reshape(128, 88 * 64))
        dnp_s = np.ascontiguousarray(dnpk[:, 88 * r:88 * (r + 1)]).view(np.uint16)
        dnp_s = np.ascontiguousarray(
            dnp_s.reshape(16, 128, 176).transpose(1, 0, 2).reshape(128, 16 * 176))
        gusv = np.ascontiguousarray(
            gus[2 * r:2 * r + 2, :].T.reshape(88, 128, 2).transpose(1, 0, 2)
            .reshape(128, 176))
        segidx = (11 * r + np.arange(11)) // 2
        dnsT = dns[segidx, :].T
        dnsv = np.ascontiguousarray(
            dnsT.reshape(16, 128, 11).transpose(1, 0, 2).reshape(128, 176))
        maps.append({
            "x_own": np.ascontiguousarray(x[b, TQ * g:TQ * (g + 1)]),
            "wq_in": np.ascontiguousarray(Wq[512 * g:512 * (g + 1)].T),
            "wk_in": WkT, "wv_in": WvT, "wo_in": WoT,
            "gup": gup_s, "dnp": dnp_s,
            "gusv": gusv, "dnsv": dnsv,
            "cvin": consts,
            "meta": np.array([[256 * g * TQ, TQ * g]], np.int32),
        })
    return maps


def kernel(L0_attn_x, L0_attn_Wq, L0_attn_Wk, L0_attn_Wv, L0_attn_Wo,
           L0_ffn_gate_up_packed, L0_ffn_gate_up_scales,
           L0_ffn_down_packed, L0_ffn_down_scales):
    x = np.ascontiguousarray(np.asarray(L0_attn_x, dtype=np.float32))
    Wq = np.ascontiguousarray(np.asarray(L0_attn_Wq, dtype=np.float32))
    Wk = np.ascontiguousarray(np.asarray(L0_attn_Wk, dtype=np.float32))
    Wv = np.ascontiguousarray(np.asarray(L0_attn_Wv, dtype=np.float32))
    Wo = np.ascontiguousarray(np.asarray(L0_attn_Wo, dtype=np.float32))
    gupk = np.ascontiguousarray(np.asarray(L0_ffn_gate_up_packed, dtype=np.int32))
    gus = np.ascontiguousarray(np.asarray(L0_ffn_gate_up_scales, dtype=np.float16))
    dnpk = np.ascontiguousarray(np.asarray(L0_ffn_down_packed, dtype=np.int32))
    dns = np.ascontiguousarray(np.asarray(L0_ffn_down_scales, dtype=np.float16))

    B, S, _ = x.shape
    TQ = S * 2 // 8
    nc = build_nc(S=S)
    maps = _in_maps(x, Wq, Wk, Wv, Wo, gupk, gus, dnpk, dns)
    res = run_bass_kernel_spmd(nc, maps, core_ids=list(range(8)))
    out = np.empty((B, S, DM), np.float32)
    for r in range(8):
        b, g = r // 4, r % 4
        out[b, TQ * g:TQ * (g + 1)] = np.asarray(res.results[r]["out_own"])
    return out


# version 21
# speedup vs baseline: 1.0539x; 1.0539x over previous
from contextlib import ExitStack

import numpy as np

import concourse.bass as bass
import concourse.mybir as mybir
from concourse.bass_utils import run_bass_kernel_spmd

F32 = mybir.dt.float32
BF16 = mybir.dt.bfloat16
F16 = mybir.dt.float16
I32 = mybir.dt.int32
U16 = mybir.dt.uint16
AF = mybir.ActivationFunctionType
ALU = mybir.AluOpType

DM = 2048
NF = DM // 128
II = 5632
SHIFT = 16.0
MASKBIG = 60000.0

ENGINES = ("sync", "scalar", "vector", "tensor", "gpsimd")
COMPUTE_SEMS = ("pe", "act", "dve", "pool", "ccx", "cck", "ccg", "ccd", "cco")
SINGLE_DMA_SEMS = tuple(f"dq{i}" for i in range(12)) + (
    "gkv", "go", "wkc", "wvc", "wqc", "woc",
    "wq0", "wq1", "wq2", "wq3") + tuple(f"sg{i}" for i in range(10))
MULTI_DMA_SEMS = ("ld0", "ld1", "sn0", "sn1", "sxn", "skv0", "skv1", "kfm", "vtm",
                  "mst0", "mst1", "mld0", "mld1", "mld2", "mld3",
                  "zs0", "zs1", "zl0", "zl1",
                  "xg0", "xg1", "ost", "xp0", "xp1", "ou0", "ou1",
                  "so0", "so1", "wb0", "wb1", "xfm", "wts", "os0", "os1") + \
                 tuple(f"dn{i}" for i in range(8))


def _flat(after):
    out = []
    for t in after:
        if t is None:
            continue
        if isinstance(t, tuple) and len(t) == 2 and isinstance(t[0], str):
            out.append(t)
        elif isinstance(t, (list, tuple)):
            out.extend(_flat(t))
    return out


class Prog:
    def __init__(self, nc):
        self.nc = nc
        self.qs = {e: [] for e in ENGINES}
        self.semnames = list(COMPUTE_SEMS) + list(SINGLE_DMA_SEMS) + \
            list(MULTI_DMA_SEMS)
        self.cnt = {s: 0 for s in self.semnames}
        self.sat = {e: {s: 0 for s in self.semnames} for e in ENGINES}
        self.sem = {}

    def alloc_sems(self, ctx):
        for s in self.semnames:
            self.sem[s] = ctx.enter_context(self.nc.semaphore(s))

    def op(self, eng, fn, after=(), sem="auto", inc=None):
        if sem == "auto":
            sem = {"tensor": "pe", "scalar": "act", "vector": "dve",
                   "gpsimd": "pool"}[eng]
        waits = []
        for s, v in _flat(after):
            if v > self.sat[eng][s]:
                self.sat[eng][s] = v
                waits.append((s, v))
        amt = inc if inc is not None else (1 if sem in COMPUTE_SEMS else 16)
        val = None
        if sem is not None:
            self.cnt[sem] += amt
            val = self.cnt[sem]

        def clo(e, waits=waits, fn=fn, sem=sem, amt=amt):
            for s, v in waits:
                e.wait_ge(self.sem[s], v)
            ins = fn(e)
            if sem is not None:
                ins.then_inc(self.sem[sem], amt)

        self.qs[eng].append(clo)
        return (sem, val) if sem is not None else None

    def wait(self, eng, after):
        waits = []
        for s, v in _flat(after):
            if v > self.sat[eng][s]:
                self.sat[eng][s] = v
                waits.append((s, v))
        if waits:
            def clo(e, waits=waits):
                for s, v in waits:
                    e.wait_ge(self.sem[s], v)
            self.qs[eng].append(clo)

    def barrier(self, exclude=()):
        toks = [(s, self.cnt[s]) for s in self.semnames if s not in exclude]
        for e in ENGINES:
            self.wait(e, toks)

    def emit(self):
        with self.nc.Block() as block:
            @block.sync
            def _(e):
                for f in self.qs["sync"]:
                    f(e)

            @block.scalar
            def _(e):
                for f in self.qs["scalar"]:
                    f(e)

            @block.vector
            def _(e):
                for f in self.qs["vector"]:
                    f(e)

            @block.tensor
            def _(e):
                for f in self.qs["tensor"]:
                    f(e)

            @block.gpsimd
            def _(e):
                for f in self.qs["gpsimd"]:
                    f(e)


def build_nc(S=4096):
    TQ = S * 2 // 8
    assert TQ % 512 == 0
    NTT = TQ // 128
    NQB = TQ // 512
    NQT = S // 512
    NOFF = 4 * (NQT - 1) + 4
    GUC2 = 64
    DNC2 = 176
    NIT = 88
    NOT_ = 16
    DNI = 704

    nc = bass.Bass("TRN2", target_bir_lowering=False, num_devices=8)

    x_own = nc.dram_tensor("x_own", [TQ, DM], F32, kind="ExternalInput").ap()
    wq_in = nc.dram_tensor("wq_in", [DM, 512], F32, kind="ExternalInput").ap()
    wk_in = nc.dram_tensor("wk_in", [DM, 512], F32, kind="ExternalInput").ap()
    wv_in = nc.dram_tensor("wv_in", [DM, 512], F32, kind="ExternalInput").ap()
    wo_in = nc.dram_tensor("wo_in", [DM, DM], F32, kind="ExternalInput").ap()
    gup = nc.dram_tensor("gup", [128, NIT * GUC2], U16, kind="ExternalInput").ap()
    dnp = nc.dram_tensor("dnp", [128, NOT_ * DNC2], U16, kind="ExternalInput").ap()
    gusv = nc.dram_tensor("gusv", [128, 2 * NIT], F16, kind="ExternalInput").ap()
    dnsv = nc.dram_tensor("dnsv", [128, 11 * NOT_], F16, kind="ExternalInput").ap()
    cvin = nc.dram_tensor("cvin", [128, 8], F32, kind="ExternalInput").ap()
    meta = nc.dram_tensor("meta", [1, 2], I32, kind="ExternalInput").ap()
    out_own = nc.dram_tensor("out_own", [TQ, DM], F32, kind="ExternalOutput").ap()

    dr = lambda n, s, d: nc.dram_tensor(n, s, d, kind="Internal").ap()
    xn_tm = dr("xn_tm", [TQ, DM], BF16)
    w_dec_gu = dr("w_dec_gu", [2 * II, DM // 8], BF16)
    w_dec_dn = dr("w_dec_dn", [DM, DNI], BF16)
    send_xn = dr("send_xn", [DM, TQ], BF16)
    gxn = dr("gxn", [4 * DM, TQ], BF16)
    send_kv = dr("send_kv", [1024, TQ], BF16)
    g_kv = dr("g_kv", [4096, TQ], BF16)
    recv_kv = dr("recv_kv", [1024, TQ], BF16)
    send_wgu = dr("send_wgu", [DM // 8, 2 * II], BF16)
    g_wgu = nc.dram_tensor("g_wgu", [DM, 2 * II], BF16, kind="Internal",
                           addr_space="Shared").ap()
    send_wdn = dr("send_wdn", [DNI, DM], BF16)
    g_wdn = nc.dram_tensor("g_wdn", [II, DM], BF16, kind="Internal",
                           addr_space="Shared").ap()
    m_dram = dr("m_dram", [128, NOFF * 512], F16)
    z_dram = dr("z_dram", [1, 2 * 512], F32)
    w_dram = dr("w_dram", [DM, DM], BF16)
    send_o = dr("send_o", [512, S], BF16)
    g_o = dr("g_o", [2048, S], BF16)
    recv_o = dr("recv_o", [2048, TQ], BF16)

    ctx = ExitStack()
    sb = lambda n, s, d: ctx.enter_context(nc.sbuf_tensor(n, s, d))

    CONSTS = [0.0, 1.0, -float(SHIFT), 1.0 / DM, 1e-5, 1e-6]
    cpool = sb("cpool", [128, len(CONSTS)], F32)
    for i, cv in enumerate(CONSTS):
        nc.const_aps.register(F32, cv, cpool[:, i:i + 1])
    cp = sb("cp", [128, 8], F32)
    meta_sb = sb("meta_sb", [1, 2], I32)
    xn_fm = sb("xn_fm", [128, NF * TQ], BF16)
    id16 = sb("id16", [128, 128], F16)
    ones_col = sb("ones_col", [128, 1], BF16)
    ones_row = sb("ones_row", [1, 128], BF16)
    onesf16 = sb("onesf16", [128, 128], F16)
    amt16 = sb("amt16", [128, DNI], U16)
    gusf = sb("gusf", [128, 2 * NIT], F32)
    dnsf = sb("dnsf", [128, 11 * NOT_], F32)
    psP = [ctx.enter_context(nc.psum_tensor(f"psP{i}", [128, 1024], F32))
           for i in range(4)]
    psA = [psP[i // 2][:, (i % 2) * 512:(i % 2 + 1) * 512] for i in range(8)]

    prog = Prog(nc)
    prog.alloc_sems(ctx)
    P = prog.op
    group4 = [[0, 1, 2, 3], [4, 5, 6, 7]]
    group8 = [list(range(8))]

    tok_consts = []
    for i, cv in enumerate(CONSTS):
        tok_consts.append(P("gpsimd", lambda e, i=i, cv=cv: e.memset(
            cpool[:, i:i + 1], cv)))
    t_on1 = P("gpsimd", lambda e: e.memset(ones_col[:, :], 1.0))
    t_on2 = P("gpsimd", lambda e: e.memset(ones_row[:, :], 1.0))
    t_on3 = P("gpsimd", lambda e: e.memset(onesf16[:, :], 1.0))
    t_id = P("gpsimd", lambda e: e.affine_select(
        id16[:, :], onesf16[:, :], [[1, 128]], ALU.is_equal, 0.0,
        base=0, channel_multiplier=-1), after=[t_on3])
    t_amt = P("gpsimd", lambda e: e.iota(
        amt16[:, :], [[0, DNI // 4], [4, 4]], base=0, channel_multiplier=0,
        allow_small_or_imprecise_dtypes=True))
    tok_consts += [t_on1, t_on2, t_id, t_amt]

    sP = ExitStack()
    sPb = lambda n, s, d: sP.enter_context(nc.sbuf_tensor(n, s, d))
    sw_gu = sPb("sw_gu", [128, NIT * GUC2], U16)
    sw_dn = sPb("sw_dn", [128, NOT_ * DNC2], U16)
    gus16 = sPb("gus16", [128, 2 * NIT], F16)
    dns16 = sPb("dns16", [128, 11 * NOT_], F16)
    P("scalar", lambda e: e.dma_start(sw_gu[:, :], gup), sem="ld1")
    P("scalar", lambda e: e.dma_start(sw_dn[:, :], dnp), sem="ld1")
    P("scalar", lambda e: e.dma_start(gus16[:, :], gusv), sem="ld1")
    t_ld1 = P("scalar", lambda e: e.dma_start(dns16[:, :], dnsv), sem="ld1")

    s0 = ExitStack()
    s0b = lambda n, s, d: s0.enter_context(nc.sbuf_tensor(n, s, d))
    x_sb = s0b("x_sb", [128, NTT * DM], F32)
    xn_ping = s0b("xn_ping", [128, 2 * DM], BF16)
    dump0 = s0b("dump0", [128, DM], F32)
    stat = s0b("stat", [128, 6 * NTT], F32)

    P("sync", lambda e: e.dma_start(cp[:, :], cvin), sem="ld0")
    P("sync", lambda e: e.dma_start(meta_sb[:, :], meta), sem="ld0")
    t_ld0 = P("sync", lambda e: e.dma_start(
        bass.AP(x_sb, 0, [[NTT * DM, 128], [DM, NTT], [1, DM]]),
        bass.AP(x_own.tensor, 0, [[DM, 128], [128 * DM, NTT], [1, DM]])),
        sem="ld0")

    t_xn_st = []
    sn_last = [None, None]
    prev_sq = None
    for i in range(NTT):
        c0 = 6 * i
        xt = x_sb[:, i * DM:(i + 1) * DM]
        ts1 = P("vector", lambda e, xt=xt, c0=c0: e.reduce_sum(
            stat[:, c0:c0 + 1], xt, mybir.AxisListType.X), after=[t_ld0])
        ts2 = P("scalar", lambda e, xt=xt, c0=c0: e.activation(
            dump0[:, :], xt, AF.Square, accum_out=stat[:, c0 + 1:c0 + 2]),
            after=[t_ld0, prev_sq] + tok_consts)
        ts3 = P("vector", lambda e, c0=c0: e.tensor_single_scalar(
            stat[:, c0 + 2:c0 + 3], stat[:, c0:c0 + 1], 1.0 / DM, ALU.mult),
            after=[ts1])
        ts4 = P("vector", lambda e, c0=c0: e.tensor_tensor(
            stat[:, c0 + 3:c0 + 4], stat[:, c0 + 2:c0 + 3],
            stat[:, c0 + 2:c0 + 3], ALU.mult), after=[ts3])
        ts5 = P("vector", lambda e, c0=c0: e.tensor_single_scalar(
            stat[:, c0 + 4:c0 + 5], stat[:, c0 + 1:c0 + 2], 1.0 / DM, ALU.mult),
            after=[ts2])
        ts6 = P("vector", lambda e, c0=c0: e.tensor_tensor(
            stat[:, c0 + 3:c0 + 4], stat[:, c0 + 4:c0 + 5],
            stat[:, c0 + 3:c0 + 4], ALU.subtract), after=[ts4, ts5])
        ts7 = P("scalar", lambda e, c0=c0: e.activation(
            stat[:, c0 + 5:c0 + 6], stat[:, c0 + 3:c0 + 4], AF.Sqrt,
            bias=1e-5, scale=1.0), after=[ts6])
        prev_sq = ts7
        ts8 = P("vector", lambda e, c0=c0: e.reciprocal(
            stat[:, c0 + 5:c0 + 6], stat[:, c0 + 5:c0 + 6]), after=[ts7])
        pp = i % 2
        ts9 = P("vector", lambda e, xt=xt, c0=c0, pp=pp: e.tensor_scalar(
            xn_ping[:, pp * DM:(pp + 1) * DM], xt,
            stat[:, c0 + 2:c0 + 3], stat[:, c0 + 5:c0 + 6],
            ALU.subtract, ALU.mult),
            after=[ts8, t_xn_st[i - 2] if i >= 2 else None])
        tok = P("sync", lambda e, i=i, pp=pp: e.dma_start(
            xn_tm[i * 128:(i + 1) * 128, :], xn_ping[:, pp * DM:(pp + 1) * DM]),
            after=[ts9], sem=f"sn{pp}")
        sn_last[pp] = tok
        t_xn_st.append(tok)

    for fi in range(NF):
        P("scalar", lambda e, fi=fi: e.dma_start(
            xn_fm[:, fi * TQ:(fi + 1) * TQ],
            xn_tm[:, fi * 128:(fi + 1) * 128], transpose=True),
            after=sn_last, sem="xfm")
    t_xfm = ("xfm", prog.cnt["xfm"])
    t_sendxn_last = None
    for fi in range(NF):
        t_sendxn_last = P("sync", lambda e, fi=fi: e.dma_start(
            send_xn[fi * 128:(fi + 1) * 128, :],
            xn_fm[:, fi * TQ:(fi + 1) * TQ]),
            after=[t_xfm], sem="sxn")
    t_ccx = P("gpsimd", lambda e: e.collective_compute(
        "AllGather", ALU.bypass, replica_groups=group4,
        ins=[send_xn.opt()], outs=[gxn.opt()]),
        after=[t_sendxn_last], sem="ccx")

    MB = 4
    NMB = NOFF // MB
    md_i32 = s0b("md_i32", [128, MB * 512], I32)
    md_f32 = s0b("md_f32", [128, MB * 512], F32)
    md_sq = s0b("md_sq", [128, MB * 512], F32)
    md_m32 = s0b("md_m32", [128, MB * 512], F32)
    m_tile = s0b("m_tile", [128, 2 * MB * 512], F16)
    pv = {}
    mst_toks = []
    for mb in range(NMB):
        offs = (mb * MB - 3) * 128
        ti = P("gpsimd", lambda e, offs=offs: e.iota(
            md_i32[:, :], [[128, MB], [1, 512]], base=offs,
            channel_multiplier=-1), after=[pv.get("tc")])
        tc = P("vector", lambda e: e.tensor_copy(md_f32[:, :], md_i32[:, :]),
               after=[ti, pv.get("tm")])
        tm = P("vector", lambda e: e.tensor_single_scalar(
            md_f32[:, :], md_f32[:, :], 0.0, ALU.max), after=[tc, pv.get("tsq")])
        tsq = P("scalar", lambda e: e.activation(md_sq[:, :], md_f32[:, :],
                                                 AF.Sqrt),
                after=[tm, pv.get("tn")] + tok_consts)
        tn = P("vector", lambda e: e.tensor_single_scalar(
            md_m32[:, :], md_sq[:, :], -1.0, ALU.mult),
            after=[tsq, pv.get("tas")])
        tas = P("gpsimd", lambda e, offs=offs: e.affine_select(
            md_m32[:, :], md_m32[:, :], [[128, MB], [1, 512]], ALU.is_ge,
            -MASKBIG, base=offs, channel_multiplier=-1),
            after=[tn, pv.get("t16")])
        t16 = P("vector", lambda e, mb=mb: e.tensor_copy(
            m_tile[:, (mb % 2) * MB * 512:(mb % 2 + 1) * MB * 512],
            md_m32[:, :]),
            after=[tas, mst_toks[mb - 2] if mb >= 2 else None])
        pv = {"tc": tc, "tm": tm, "tsq": tsq, "tn": tn, "tas": tas, "t16": t16}
        mst_toks.append(P("sync", lambda e, mb=mb: e.dma_start(
            m_dram[:, mb * MB * 512:(mb + 1) * MB * 512],
            m_tile[:, (mb % 2) * MB * 512:(mb % 2 + 1) * MB * 512]),
            after=[t16], sem=f"mst{mb % 2}"))
    t_mst = [("mst0", prog.cnt["mst0"]), ("mst1", prog.cnt["mst1"])]

    s0_toks = [(s, prog.cnt[s]) for s in prog.semnames if s != "ccx"]
    for _e in ("sync", "scalar", "vector", "gpsimd"):
        prog.wait(_e, s0_toks)
    s0.close()

    attc = ExitStack()
    ab = lambda n, s, d: attc.enter_context(nc.sbuf_tensor(n, s, d))
    q_sb = ab("q_sb", [128, 4 * S], BF16)
    k_fm = ab("k_fm", [128, S], BF16)
    v_tm = ab("v_tm", [128, S], BF16)
    p_sb = ab("p_sb", [128, 4 * 1024], BF16)
    szpart = ab("szpart", [128, 2 * 512], F32)
    sz16 = ab("sz16", [128, 2 * 512], BF16)
    zrec32 = ab("zrec32", [1, 2 * 512], F32)
    zrecb = ab("zrecb", [1, 2 * 512], BF16)
    o_out = ab("o_out", [128, 2 * 512], BF16)
    o_un = ab("o_un", [128, 4 * 512], F32)

    s1 = ExitStack()
    s1b = lambda n, s, d: s1.enter_context(nc.sbuf_tensor(n, s, d))
    dec_nib = s1b("dec_nib", [128, DNI], U16)
    dec_t7 = s1b("dec_t7", [128, DNI], U16)
    dec_t8 = s1b("dec_t8", [128, DNI], U16)
    dec_fx = s1b("dec_fx", [128, DNI], BF16)
    dec_vals = s1b("dec_vals", [128, 2 * 2816], BF16)

    s2 = ExitStack()
    s2b = lambda n, s, d: s2.enter_context(nc.sbuf_tensor(n, s, d))
    wkvT = s2b("wkvT", [128, NF * 1024], BF16)
    wqT = s2b("wqT", [128, NF * 512], BF16)
    kv_sb = s2b("kv_sb", [128, 2 * 512], BF16)
    xg_ping = s2b("xg_ping", [128, 2 * 2048], BF16)

    t_wkc = P("gpsimd", lambda e: e.dma_start(
        bass.AP(wkvT, 0, [[NF * 1024, 128], [1024, NF], [1, 512]]),
        bass.AP(wk_in.tensor, 0, [[512, 128], [128 * 512, NF], [1, 512]])),
        sem="wkc")
    t_wvc = P("gpsimd", lambda e: e.dma_start(
        bass.AP(wkvT, 512, [[NF * 1024, 128], [1024, NF], [1, 512]]),
        bass.AP(wv_in.tensor, 0, [[512, 128], [128 * 512, NF], [1, 512]])),
        sem="wvc")
    t_wqc = P("gpsimd", lambda e: e.dma_start(
        bass.AP(wqT, 0, [[NF * 512, 128], [512, NF], [1, 512]]),
        bass.AP(wq_in.tensor, 0, [[512, 128], [128 * 512, NF], [1, 512]])),
        sem="wqc")

    t_gusf = P("vector", lambda e: e.tensor_copy(gusf[:, :], gus16[:, :]),
               after=[t_ld1])
    t_dnsf = P("vector", lambda e: e.tensor_copy(dnsf[:, :], dns16[:, :]),
               after=[t_ld1])

    def decode_tile(handle, row_len, col_off, n_u16, scale_fn, pos, cpar,
                    t_after, eng="vector"):
        nnib = 4 * n_u16
        vo = cpar * 2816 + pos * nnib
        rep = bass.AP(handle, col_off, [[row_len, 128], [1, n_u16], [0, 4]])
        nib3 = bass.AP(dec_nib, 0, [[DNI, 128], [4, n_u16], [1, 4]])
        amt3 = bass.AP(amt16, 0, [[DNI, 128], [4, n_u16], [1, 4]])
        t1 = P(eng, lambda e: e.tensor_tensor(
            nib3, rep, amt3, ALU.logical_shift_right), after=t_after)
        t3 = P(eng, lambda e: e.tensor_scalar(
            dec_t7[:, :nnib], dec_nib[:, :nnib], 7, 6,
            ALU.bitwise_and, ALU.logical_shift_left), after=[t1])
        t5 = P(eng, lambda e: e.tensor_scalar(
            dec_t8[:, :nnib], dec_nib[:, :nnib], 8, 12,
            ALU.bitwise_and, ALU.logical_shift_left), after=[t1])
        t6 = P(eng, lambda e: e.tensor_single_scalar(
            dec_t7[:, :nnib], dec_t7[:, :nnib], 0x3F00, ALU.add), after=[t3])
        bview = bass.AP(dec_t7, 0, [[DNI, 128], [1, nnib]]).bitcast(BF16)
        t7_ = P(eng, lambda e: e.tensor_scalar(
            dec_fx[:, :nnib], bview, 1.0, 0.0, ALU.subtract, ALU.min),
            after=[t6])
        vpos = bass.AP(dec_vals, vo, [[2 * 2816, 128], [1, nnib]]).bitcast(BF16)
        t8_ = P(eng, lambda e: e.tensor_tensor(
            vpos, bview, dec_fx[:, :nnib], ALU.add), after=[t7_])
        valsu = bass.AP(dec_vals, vo, [[2 * 2816, 128], [1, nnib]]).bitcast(U16)
        t9 = P(eng, lambda e: e.tensor_tensor(
            valsu, valsu, dec_t8[:, :nnib], ALU.add), after=[t8_, t5])
        return scale_fn(t9, vo)

    dq_toks = []
    dq_store_fns = []
    prev_dve = None
    for chunk in range(8):
        chunk_dves = []
        for k in range(11):
            it = chunk * 11 + k

            def mk_scale(it=it):
                def scale_fn(tok, vo):
                    a = P("vector", lambda e, it=it, vo=vo: e.tensor_scalar(
                        dec_vals[:, vo:vo + 128], dec_vals[:, vo:vo + 128],
                        gusf[:, 2 * it:2 * it + 1], None, ALU.mult),
                        after=[tok, t_gusf])
                    return P("vector", lambda e, it=it, vo=vo: e.tensor_scalar(
                        dec_vals[:, vo + 128:vo + 256],
                        dec_vals[:, vo + 128:vo + 256],
                        gusf[:, 2 * it + 1:2 * it + 2], None, ALU.mult),
                        after=[a])
                return scale_fn
            prev_dve = decode_tile(
                sw_gu, NIT * GUC2, it * GUC2, GUC2, mk_scale(), k, chunk % 2,
                [t_ld1, t_amt, prev_dve,
                 dq_toks[chunk - 2] if chunk >= 2 and k == 0 else None])
            chunk_dves.append(prev_dve)
        ci = chunk
        tok = ("dq%d" % ci, 16)
        prog.cnt["dq%d" % ci] = 0

        def store_fn(e, ci=ci):
            return e.dma_start(
                bass.AP(w_dec_gu.tensor, ci * 11 * 128 * 256,
                        [[256, 128], [128 * 256, 11], [1, 256]]),
                bass.AP(dec_vals, (ci % 2) * 2816,
                        [[2 * 2816, 128], [256, 11], [1, 256]]))
        dq_store_fns.append(("dq%d" % ci, store_fn, list(chunk_dves)))
        dq_toks.append(tok)

    t_dgu_dve_last = prev_dve
    for semname, fn, deps in dq_store_fns[:6]:
        P("gpsimd", fn, after=deps, sem=semname)
    t_dgu_done = dq_toks[:8]

    for chunk in range(4):
        chunk_dves = []
        for k in range(4):
            ot = chunk * 4 + k

            def mk_scale_dn(ot=ot):
                def scale_fn(tok, vo):
                    last = tok
                    for s in range(11):
                        last = P("vector", lambda e, s=s, ot=ot, vo=vo:
                                 e.tensor_scalar(
                            dec_vals[:, vo + s * 64:vo + (s + 1) * 64],
                            dec_vals[:, vo + s * 64:vo + (s + 1) * 64],
                            dnsf[:, 11 * ot + s:11 * ot + s + 1], None,
                            ALU.mult), after=[last, t_dnsf])
                    return last
                return scale_fn
            prev_dve = decode_tile(
                sw_dn, NOT_ * DNC2, ot * DNC2, DNC2, mk_scale_dn(), k,
                chunk % 2,
                [t_ld1, t_amt, prev_dve,
                 (dq_toks[8 + chunk - 2] if chunk >= 2 else dq_toks[6 + chunk])
                 if k == 0 else None])
            chunk_dves.append(prev_dve)
        ci = 8 + chunk

        def store_fn(e, chunk=chunk):
            return e.dma_start(
                bass.AP(w_dec_dn.tensor, chunk * 4 * 128 * DNI,
                        [[DNI, 128], [128 * DNI, 4], [1, DNI]]),
                bass.AP(dec_vals, (chunk % 2) * 2816,
                        [[2 * 2816, 128], [704, 4], [1, 704]]))
        dq_store_fns.append(("dq%d" % ci, store_fn, list(chunk_dves)))
        dq_toks.append(("dq%d" % ci, 16))

    t_ddn_dve_last = prev_dve
    t_ddn_done = dq_toks[8:]





    kv_free = [None, None]
    kv_sent = [None, None]
    t_kv_send_last = None
    for tb in range(NQB):
        for c8 in range(8):
            bank = (tb * 8 + c8) % 2
            off0 = c8 * 128 if c8 < 4 else 512 + (c8 - 4) * 128
            last = None
            for fi in range(NF):
                a = [t_wkc if c8 < 4 else t_wvc, t_xfm]
                if fi == 0:
                    a.append(kv_free[bank])
                last = P("tensor", lambda e, fi=fi, off0=off0, tb=tb, bank=bank:
                         e.matmul(
                    psA[bank],
                    wkvT[:, fi * 1024 + off0: fi * 1024 + off0 + 128],
                    xn_fm[:, fi * TQ + tb * 512: fi * TQ + (tb + 1) * 512],
                    start=(fi == 0), stop=(fi == NF - 1)),
                    after=a, sem="pe" if fi == NF - 1 else None)
            ev = P("scalar", lambda e, bank=bank: e.activation(
                kv_sb[:, bank * 512:(bank + 1) * 512],
                psA[bank], AF.Copy, bias=0.0, scale=1.0),
                after=[last, kv_sent[bank]] + tok_consts)
            kv_free[bank] = ev
            t_kv_pe_last = last
            row = 256 * (c8 % 4) + (0 if c8 < 4 else 128)
            kv_sent[bank] = P("sync", lambda e, tb=tb, row=row, bank=bank:
                              e.dma_start(
                send_kv[row:row + 128, tb * 512:(tb + 1) * 512],
                kv_sb[:, bank * 512:(bank + 1) * 512]),
                after=[ev], sem=f"skv{bank}")

    t_cck = P("gpsimd", lambda e: e.collective_compute(
        "AllGather", ALU.bypass, replica_groups=group4,
        ins=[send_kv.opt()], outs=[g_kv.opt()]),
        after=[kv_sent[0], kv_sent[1]], sem="cck")
    for semname, fn, deps in dq_store_fns[6:]:
        P("gpsimd", fn, after=deps, sem=semname)

    q_free = [None] * 4
    q_last = [None] * 4
    xg_free = [None, None]
    t_xg_last = [None, None]
    for tb in range(S // 512):
        tq, half = tb // NQB, tb % NQB
        for fq in range(4):
            pp = (tb * 4 + fq) % 2
            txg = P("sync", lambda e, tq=tq, half=half, fq=fq, pp=pp:
                    e.dma_start(
                bass.AP(xg_ping, pp * 2048,
                        [[2 * 2048, 128], [512, 4], [1, 512]]),
                bass.AP(gxn.tensor,
                        (DM * tq + fq * 512) * TQ + half * 512,
                        [[TQ, 128], [128 * TQ, 4], [1, 512]])),
                after=[t_ccx, xg_free[pp]], sem=f"xg{pp}")
            t_xg_last[pp] = txg
            for fj in range(4):
                fi = fq * 4 + fj
                for hl in range(4):
                    need_tok = (fi == NF - 1) or (hl == 3 and fj == 3)
                    a = [txg, t_wqc]
                    if fi == 0:
                        a.append(q_free[hl])
                    tok = P("tensor", lambda e, fi=fi, fj=fj, hl=hl, pp=pp:
                            e.matmul(
                        psA[2 + hl],
                        wqT[:, fi * 512 + hl * 128: fi * 512 + (hl + 1) * 128],
                        xg_ping[:, pp * 2048 + fj * 512: pp * 2048 + (fj + 1) * 512],
                        start=(fi == 0), stop=(fi == NF - 1)),
                        after=a, sem="pe" if need_tok else None)
                    if hl == 3 and fj == 3:
                        xg_free[pp] = tok
                    if need_tok:
                        t_q_pe_last = tok
                    if fi == NF - 1:
                        q_free[hl] = P("scalar", lambda e, hl=hl, tb=tb:
                                       e.activation(
                            q_sb[:, hl * S + tb * 512: hl * S + (tb + 1) * 512],
                            psA[2 + hl], AF.Copy, bias=0.0,
                            scale=cp[:, hl:hl + 1]),
                            after=[tok, t_ld0] + tok_consts)
                        q_last[hl] = q_free[hl]

    def kv_slice(e):
        with e.register("rkv") as rkv:
            e.reg_load(rkv, meta_sb[0:1, 0:1])
            return e.dma_start(
                recv_kv,
                bass.AP(g_kv.tensor, rkv, [[1024 * TQ, 4], [TQ, 256], [1, TQ]]))
    t_gkv = P("sync", kv_slice, after=[t_cck, t_ld0], sem="gkv")

    t_kfm_last = None
    for tq in range(4):
        t_kfm_last = P("sync", lambda e, tq=tq: e.dma_start(
            k_fm[:, tq * TQ:(tq + 1) * TQ], recv_kv[256 * tq:256 * tq + 128, :]),
            after=[t_gkv], sem="kfm")

    t_vtm_last = None
    for tq in range(4):
        for tl in range(TQ // 128):
            j = tq * (TQ // 128) + tl
            t_vtm_last = P("sync", lambda e, tq=tq, tl=tl, j=j: e.dma_start(
                v_tm[:, j * 128:(j + 1) * 128],
                recv_kv[256 * tq + 128:256 * tq + 256, tl * 128:(tl + 1) * 128],
                transpose=True), after=[t_gkv], sem="vtm")

    s2.close()
    attc2 = ExitStack()
    ab2 = lambda n, s, d: attc2.enter_context(nc.sbuf_tensor(n, s, d))
    m_sb = ab2("m_sb", [128, NOFF * 512], F16)
    wt2 = ab2("wt2", [128, II], BF16)
    reuse_toks = [t_q_pe_last, t_kv_pe_last, kv_sent[0], kv_sent[1],
                  t_xg_last[0], t_xg_last[1], t_wkc, t_wvc, t_wqc,
                  q_free[0], q_free[1], q_free[2], q_free[3]]
    t_m = []
    NMC = (NOFF + 7) // 8
    for mc in range(NMC):
        tl = P("sync", lambda e, mc=mc: e.dma_start(
            m_sb[:, mc * 4096:(mc + 1) * 4096],
            m_dram[:, mc * 4096:(mc + 1) * 4096]),
            after=reuse_toks + t_mst, sem=f"mld{mc}")
        t_m += [tl] * 8
    t_m = t_m[:NOFF]

    swg_fns = []
    t_wts_prev = None
    sgi = 0
    for fc in range(2):
        for half in range(2):
            t_wts_prev = P("sync", lambda e, fc=fc, half=half: e.dma_start(
                wt2[:, :],
                w_dec_gu[half * II:(half + 1) * II, fc * 128:(fc + 1) * 128],
                transpose=True),
                after=t_dgu_done + reuse_toks +
                [("sg%d" % (sgi - 1), 16) if sgi else None],
                sem="wts")
            swg_fns.append(("sg%d" % sgi, fc, half, t_wts_prev))
            sgi += 1
    t_swg = []
    for semname, fc, half, dep in swg_fns:
        t_swg.append(P("gpsimd", lambda e, fc=fc, half=half: e.dma_start(
            bass.AP(send_wgu.tensor, fc * 128 + half * 44 * 256,
                    [[22528, 128], [256, 44], [1, 128]]),
            bass.AP(wt2, 0, [[II, 128], [128, 44], [1, 128]])),
            after=[dep], sem=semname))
    t_ccg = P("gpsimd", lambda e: e.collective_compute(
        "AllGather", ALU.bypass, replica_groups=group8,
        ins=[send_wgu.opt()], outs=[g_wgu.opt()]), after=t_swg, sem="ccg")

    starts = list(range(0, DNI - 127, 128))
    if starts[-1] + 128 < DNI:
        starts.append(DNI - 128)
    prev_end = 0
    dn_sends = []
    for p0 in starts:
        skip = max(0, prev_end - p0)
        prev_end = p0 + 128
        t_wts_prev = P("sync", lambda e, p0=p0: e.dma_start(
            wt2[0:128, 0:DM], w_dec_dn[:, p0:p0 + 128], transpose=True),
            after=t_ddn_done + [("sg%d" % (sgi - 1), 16)], sem="wts")
        dn_sends.append((("sg%d" % sgi), p0, skip, t_wts_prev))
        sgi += 1
    t_swd = []
    for semname, p0, skip, dep in dn_sends:
        t_swd.append(P("gpsimd", lambda e, p0=p0, skip=skip: e.dma_start(
            send_wdn[p0 + skip:p0 + 128, :], wt2[skip:128, 0:DM]),
            after=[dep], sem=semname))
    t_ccd = P("gpsimd", lambda e: e.collective_compute(
        "AllGather", ALU.bypass, replica_groups=group8,
        ins=[send_wdn.opt()], outs=[g_wdn.opt()]), after=t_swd, sem="ccd")



    blocks = [(hl, qi) for hl in range(4) for qi in range(NQT)]
    st = []
    for p in range(2):
        st.append({
            "pss": psP[p],
            "pso": psA[4 + p],
            "po": p * 2048,
            "zo": p * 512,
            "oo": p * 512,
            "s_free": [kv_free[0], kv_free[1]] if p == 0
                      else [q_free[0], q_free[1]],
            "o_free": q_free[2 + p],
            "t_z": None,
            "t_av": None,
            "so_last": None,
            "p_free": [None, None],
            "oun_free": [None, None],
            "nfin": 0,
        })
    pszS = psA[6]
    pszR = psA[7]
    zf = {}

    def emit_score(p, hl, qi, pr):
        s = st[p]
        sl = pr % 2
        s["slot"] = sl
        po = s["po"] + sl * 1024
        for u in range(2):
            k = 2 * pr + u
            moff = 4 * qi - k + 3
            P("tensor", lambda e, s=s, u=u, moff=moff: e.matmul(
                s["pss"][:, u * 512:(u + 1) * 512], id16[:, :],
                m_sb[:, moff * 512:(moff + 1) * 512],
                start=True, stop=False),
                after=[t_m[moff], s["s_free"], t_id], sem=None)
            tm_ = P("tensor", lambda e, s=s, u=u, k=k, hl=hl, qi=qi: e.matmul(
                s["pss"][:, u * 512:(u + 1) * 512],
                k_fm[:, k * 128:(k + 1) * 128],
                q_sb[:, hl * S + qi * 512:hl * S + (qi + 1) * 512],
                start=False, stop=True),
                after=[t_kfm_last, q_last[hl]],
                sem="pe" if u == 1 else None)
        t_p = P("scalar", lambda e, s=s, hl=hl, po=po: e.activation(
            p_sb[:, po:po + 1024], s["pss"][:, :],
            AF.Exp, bias=-float(SHIFT), scale=cp[:, 4 + hl:5 + hl]),
            after=[tm_, s["p_free"][sl], t_ld0] + tok_consts)
        s["s_free"] = t_p
        if pr == 0:
            s["t_z"] = P("vector", lambda e, s=s, po=po: e.tensor_tensor(
                szpart[:, s["zo"]:s["zo"] + 512],
                p_sb[:, po:po + 512],
                p_sb[:, po + 512:po + 1024], ALU.add),
                after=[t_p, s.get("tz16_prev")])
        else:
            tza = P("vector", lambda e, s=s, po=po: e.tensor_tensor(
                szpart[:, s["zo"]:s["zo"] + 512],
                szpart[:, s["zo"]:s["zo"] + 512],
                p_sb[:, po:po + 512], ALU.add),
                after=[t_p, s["t_z"]])
            s["t_z"] = P("vector", lambda e, s=s, po=po: e.tensor_tensor(
                szpart[:, s["zo"]:s["zo"] + 512],
                szpart[:, s["zo"]:s["zo"] + 512],
                p_sb[:, po + 512:po + 1024], ALU.add),
                after=[tza])
        s["t_p"] = t_p

    def emit_av(p, hl, qi, pr, nkt):
        s = st[p]
        t_p = s["t_p"]
        sl = s["slot"]
        po = s["po"] + sl * 1024
        for u in range(2):
            k = 2 * pr + u
            s["t_av"] = P("tensor", lambda e, s=s, u=u, k=k, nkt=nkt, po=po:
                          e.matmul(
                s["pso"], v_tm[:, k * 128:(k + 1) * 128],
                p_sb[:, po + u * 512:po + (u + 1) * 512],
                start=(k == 0), stop=(k == nkt - 1)),
                after=[t_p, t_vtm_last, s["o_free"] if k == 0 else None],
                sem="pe" if u == 1 else None)
        s["p_free"][sl] = [s["t_av"], s["t_z"]]

    def emit_fin1(p, hl, qi):
        s = st[p]
        pa = s["nfin"] % 2
        oev = P("scalar", lambda e, s=s, pa=pa: e.activation(
            o_un[:, (2 * p + pa) * 512:(2 * p + pa + 1) * 512], s["pso"],
            AF.Copy, bias=0.0, scale=1.0),
            after=[s["t_av"], s["oun_free"][pa]] + tok_consts)
        s["o_free"] = oev
        s["oev"] = oev
        tz16 = P("vector", lambda e, s=s: e.tensor_copy(
            sz16[:, s["zo"]:s["zo"] + 512], szpart[:, s["zo"]:s["zo"] + 512]),
            after=[s["t_z"], s.get("tzs_prev")])
        s["tz16_prev"] = tz16

    def emit_fin2a(p, hl, qi):
        s = st[p]
        tzs = P("tensor", lambda e, s=s: e.matmul(
            pszS[0:1, :], ones_col[:, :], sz16[:, s["zo"]:s["zo"] + 512],
            start=True, stop=True),
            after=[s["tz16_prev"], t_on1, zf.get("tzr0")])
        s["tzs_prev"] = tzs
        tzr0 = P("vector", lambda e, s=s: e.reciprocal(
            zrec32[0:1, s["zo"]:s["zo"] + 512], pszS[0:1, :]),
            after=[tzs, s.get("tzr_prev")])
        zf["tzr0"] = tzr0
        s["tzr_prev"] = P("vector", lambda e, s=s: e.tensor_copy(
            zrecb[0:1, s["zo"]:s["zo"] + 512],
            zrec32[0:1, s["zo"]:s["zo"] + 512]), after=[tzr0])

    def emit_fin2b(p, hl, qi):
        s = st[p]
        pa = s["nfin"] % 2
        s["nfin"] += 1
        tzrep = P("tensor", lambda e, s=s: e.matmul(
            pszR, ones_row[:, :], zrecb[0:1, s["zo"]:s["zo"] + 512],
            start=True, stop=True),
            after=[s["tzr_prev"], t_on2, zf.get("tev")])
        tev = P("vector", lambda e, s=s, pa=pa: e.tensor_tensor(
            o_out[:, s["oo"]:s["oo"] + 512],
            o_un[:, (2 * p + pa) * 512:(2 * p + pa + 1) * 512],
            pszR, ALU.mult),
            after=[tzrep, s["oev"], s["so_last"]])
        zf["tev"] = tev
        s["oun_free"][pa] = tev
        row = 128 * hl
        col = qi * 512
        s["so_last"] = P("scalar", lambda e, s=s, row=row, col=col: e.dma_start(
            send_o[row:row + 128, col:col + 512],
            o_out[:, s["oo"]:s["oo"] + 512]), after=[tev], sem=f"so{p}")

    streams = []
    for p in range(2):
        seq = []
        myblocks = [blk for i, blk in enumerate(blocks)
                    if (i // NQT) % 2 == p]
        pend = []
        for (hl, qi) in myblocks:
            nkt = 4 * qi + 4
            items = []
            for pr in range(nkt // 2):
                items.append(("score", hl, qi, pr, nkt))
                items.append(("av", hl, qi, pr, nkt))
            out = [items[0]]
            if pend:
                out.append(pend.pop(0))
            out.append(items[1])
            if pend:
                out.append(pend.pop(0))
            out += items[2:]
            seq += out
            seq.append(("fin1", hl, qi, 0, 0))
            pend = [("fin2a", hl, qi, 0, 0), ("fin2b", hl, qi, 0, 0)]
        seq += pend
        streams.append(seq)
    ii = [0, 0]
    while ii[0] < len(streams[0]) or ii[1] < len(streams[1]):
        for p in range(2):
            if ii[p] < len(streams[p]):
                kind, hl, qi, pr, nkt = streams[p][ii[p]]
                ii[p] += 1
                if kind == "score":
                    emit_score(p, hl, qi, pr)
                elif kind == "av":
                    emit_av(p, hl, qi, pr, nkt)
                elif kind == "fin1":
                    emit_fin1(p, hl, qi)
                elif kind == "fin2a":
                    emit_fin2a(p, hl, qi)
                else:
                    emit_fin2b(p, hl, qi)
    so_last = [st[0]["so_last"], st[1]["so_last"]]

    t_wodr = P("gpsimd", lambda e: e.dma_start(
        bass.AP(w_dram.tensor, 0, [[DM, DM], [1, DM]]),
        bass.AP(wo_in.tensor, 0, [[DM, DM], [1, DM]])), sem="woc")
    t_cco = P("gpsimd", lambda e: e.collective_compute(
        "AllGather", ALU.bypass, replica_groups=group4,
        ins=[send_o.opt()], outs=[g_o.opt()]), after=so_last, sem="cco")

    def o_slice(e):
        with e.register("ro") as ro:
            e.reg_load(ro, meta_sb[0:1, 1:2])
            return e.dma_start(
                recv_o, bass.AP(g_o.tensor, ro, [[S, 2048], [1, TQ]]))
    t_go = P("gpsimd", o_slice, after=[t_cco, t_ld0], sem="go")

    att_toks = [(s, prog.cnt[s]) for s in prog.semnames
                if s not in ("cco", "go", "woc", "ccg", "ccd")]

    for _e in ENGINES:
        prog.wait(_e, att_toks)
    attc2.close()
    s1.close()
    attc.close()

    gctx = ExitStack()
    g_sb = gctx.enter_context(nc.sbuf_tensor("g_sb", [128, NTT * DM], F16))
    fctx = ExitStack()
    fsb = lambda n, s, d: fctx.enter_context(nc.sbuf_tensor(n, s, d))
    wblk = [fsb("wblk0", [128, 2 * 4096], BF16),
            fsb("wblk1", [128, 2 * 4096], BF16)]
    h_sb = fsb("h_sb", [128, 44 * 512], BF16)
    silu_t = fsb("silu_t", [128, 512], F32)
    y_sb = fsb("y_sb", [128, 4 * DM], F32)
    dnstr = fsb("dnstr", [128, 2 * DM], BF16)
    ydump = fsb("ydump", [128, DM], BF16)
    rstat = fsb("rstat", [128, 2], F32)

    NHALF = TQ // 512
    NTH = 4
    t_g_done = {}
    bank_free = [None] * 4
    wgu_free = [None, None]
    dn_slot_free = [None] * 8
    pass_bank_free = [None] * 4
    h_tile_free = {}
    y_tile_free = {}
    rs_chain = {}
    sdctr = [0]
    t_down_last = [None]
    h_last = [None]
    for h2 in range(NHALF):
        h_toks = []
        yev = {}

        def gate_jb(jb, h2=h2, h_toks=None):
            ppw = jb % 2
            wb = wblk[ppw]
            j0 = 2 * jb
            tgb = P("sync", lambda e, j0=j0, wb=wb: e.dma_start(
                bass.AP(wb, 0, [[2 * 4096, 128], [512, 8], [1, 512]]),
                bass.AP(g_wgu.tensor, j0 * 256,
                        [[22528, 128], [2883584, 8], [1, 512]])),
                after=[t_ccg, wgu_free[ppw]], sem=f"wb{ppw}")
            tgb = P("sync", lambda e, j0=j0, wb=wb: e.dma_start(
                bass.AP(wb, 4096, [[2 * 4096, 128], [512, 8], [1, 512]]),
                bass.AP(g_wgu.tensor, (44 + j0) * 256,
                        [[22528, 128], [2883584, 8], [1, 512]])),
                after=[t_ccg, wgu_free[ppw]], sem=f"wb{ppw}")
            for jj in range(2):
                j = j0 + jj
                bk = jj * 2
                lastA = lastB = None
                for fi in range(NF):
                    r, flo = fi // 2, fi % 2
                    woff = r * 512 + jj * 256 + flo * 128
                    aA = [tgb, t_xfm]
                    if fi == 0:
                        aA.append(bank_free[bk])
                    lastA = P("tensor", lambda e, fi=fi, wb=wb, bk=bk, h2=h2,
                              woff=woff: e.matmul(
                        psA[bk], wb[:, woff:woff + 128],
                        xn_fm[:, fi * TQ + h2 * 512: fi * TQ + h2 * 512 + 512],
                        start=(fi == 0), stop=(fi == NF - 1)),
                        after=aA, sem="pe" if fi == NF - 1 else None)
                for fi in range(NF):
                    r, flo = fi // 2, fi % 2
                    woff = 4096 + r * 512 + jj * 256 + flo * 128
                    aB = [tgb, t_xfm]
                    if fi == 0:
                        aB.append(bank_free[bk + 1])
                    lastB = P("tensor", lambda e, fi=fi, wb=wb, bk=bk, h2=h2,
                              woff=woff: e.matmul(
                        psA[bk + 1], wb[:, woff:woff + 128],
                        xn_fm[:, fi * TQ + h2 * 512: fi * TQ + h2 * 512 + 512],
                        start=(fi == 0), stop=(fi == NF - 1)),
                        after=aB, sem="pe" if fi == NF - 1 else None)
                if jj == 1:
                    wgu_free[ppw] = lastB
                tsi = P("scalar", lambda e, bk=bk: e.activation(
                    silu_t[:, :], psA[bk], AF.Silu),
                    after=[lastA, h_toks[-1] if h_toks else None] + tok_consts)
                bank_free[bk] = tsi
                th = P("vector", lambda e, j=j, bk=bk: e.tensor_tensor(
                    h_sb[:, j * 512:(j + 1) * 512], silu_t[:, :],
                    psA[bk + 1], ALU.mult),
                    after=[tsi, lastB, h_tile_free.get(j)] +
                    (dq_toks[8:] if j == 8 and h2 == 0 else []))
                bank_free[bk + 1] = th
                h_last[0] = th
                h_toks.append(th)

        def down_step(ps, j, h2=h2, h_toks=None, yev=None):
            ch, cc = ps // 2, ps % 2
            slot = sdctr[0] % 8
            sdctr[0] += 1
            tds = P("sync" if slot % 2 == 0 else "scalar",
                    lambda e, j=j, ch=ch, cc=cc, slot=slot: e.dma_start(
                dnstr[:, slot * 512:(slot + 1) * 512],
                g_wdn[j * 128:(j + 1) * 128,
                      ch * 1024 + cc * 512:ch * 1024 + (cc + 1) * 512]),
                after=[t_ccd, dn_slot_free[slot]], sem=f"dn{slot}")
            lastm = [None] * 4
            for bt in range(4):
                a = [tds, h_toks[j]]
                if j == 0:
                    a.append(pass_bank_free[bt])
                lastm[bt] = P("tensor", lambda e, j=j, bt=bt, slot=slot:
                              e.matmul(
                    psA[4 + bt],
                    h_sb[:, j * 512 + bt * 128: j * 512 + (bt + 1) * 128],
                    dnstr[:, slot * 512:(slot + 1) * 512],
                    start=(j == 0), stop=(j == 43)),
                    after=a, sem="pe" if (bt == 3 or j == 43) else None)
            dn_slot_free[slot] = lastm[3]
            t_down_last[0] = lastm[3]
            if ps == 3:
                h_tile_free[j] = lastm[3]
            if j == 43:
                for bt in range(4):
                    tev = P("vector", lambda e, bt=bt, ch=ch, cc=cc:
                            e.tensor_copy(
                        y_sb[:, bt * DM + ch * 1024 + cc * 512:
                             bt * DM + ch * 1024 + (cc + 1) * 512],
                        psA[4 + bt]),
                        after=[lastm[bt], y_tile_free.get(bt)] +
                        (t_swd if h2 == 0 and ps == 0 and bt == 0 else []))
                    pass_bank_free[bt] = tev
                    yev[(bt, ps)] = tev

        down_sched = [(ps, j) for ps in range(4) for j in range(44)]
        di = 0
        for jb in range(22):
            gate_jb(jb, h2, h_toks)
            navail = 2 * (jb + 1)
            while di < len(down_sched) and di < (jb + 1) * 8:
                ps, j = down_sched[di]
                if ps == 0 and j >= navail:
                    break
                down_step(ps, j, h2, h_toks, yev)
                di += 1
        while di < len(down_sched):
            ps, j = down_sched[di]
            down_step(ps, j, h2, h_toks, yev)
            di += 1

        for tt in range(NTH):
            tglob = h2 * NTH + tt
            tsq = P("scalar", lambda e, tt=tt: e.activation(
                ydump[:, :], y_sb[:, tt * DM:(tt + 1) * DM], AF.Square,
                accum_out=rstat[:, 0:1]),
                after=[yev[(tt, ps)] for ps in range(4)] +
                [rs_chain.get("tsr"), rs_chain.get("tyn"),
                 rs_chain.get("tg")] + tok_consts)
            tsr = P("scalar", lambda e: e.activation(
                rstat[:, 1:2], rstat[:, 0:1], AF.Sqrt, bias=1e-6,
                scale=1.0 / DM), after=[tsq, rs_chain.get("trc")])
            trc = P("vector", lambda e: e.reciprocal(rstat[:, 1:2], rstat[:, 1:2]),
                    after=[tsr])
            tyn = P("vector", lambda e, tt=tt: e.tensor_scalar(
                y_sb[:, tt * DM:(tt + 1) * DM],
                y_sb[:, tt * DM:(tt + 1) * DM], rstat[:, 1:2],
                None, ALU.mult), after=[trc, tsq, rs_chain.get("tg")])
            tg = P("scalar", lambda e, tt=tt, tglob=tglob: e.activation(
                g_sb[:, tglob * DM:(tglob + 1) * DM],
                y_sb[:, tt * DM:(tt + 1) * DM], AF.Silu),
                after=[tyn])
            y_tile_free[tt] = tyn
            rs_chain = {"tsr": tsr, "trc": trc, "tyn": tyn, "tg": tg}
            t_g_done[tglob] = tg

    pse = [psA[0], psA[1]]

    def alias(name, shape, dtype, target, off=0):
        return nc.alloc_sbuf_tensor_at(
            name, shape, dtype, offset=nc.lookup_mloc(target).addr + off)

    woq = [alias("woq0", [128, 4 * DM], BF16, wblk[0]),
           alias("woq1", [128, 4 * DM], BF16, wblk[1]),
           alias("woq2", [128, 4 * DM], BF16, h_sb),
           alias("woq3", [128, 4 * DM], BF16, h_sb, 4 * DM * 2)]
    ostrip_p = alias("ostrip_p", [128, 2 * 2048], BF16, sw_gu)
    xep = alias("xep", [128, 2 * 512], F32, sw_gu, 8192)
    etmp = alias("etmp", [128, 2 * 512], F32, sw_gu, 12288)

    wb_tot = [("wb0", prog.cnt["wb0"]), ("wb1", prog.cnt["wb1"])]
    woq_tok = []
    for i in range(4):
        gate_i = ([wgu_free[0], wgu_free[1], t_wodr] + wb_tot) if i < 2 else                  ([t_down_last[0], h_last[0], t_wodr])
        woq_tok.append(P("sync" if i % 2 == 0 else "scalar",
                         lambda e, i=i: e.dma_start(
            bass.AP(woq[i], 0, [[4 * DM, 128], [DM, 4], [1, DM]]),
            bass.AP(w_dram.tensor, i * 4 * 128 * DM,
                    [[DM, 128], [128 * DM, 4], [1, DM]])),
            after=gate_i, sem=f"wq{i}"))

    t_out_last = [None, None]
    e_ps_free = [None, None]
    etmp_free = [None, None]
    xep_free = [None, None]
    ost_free = [None, None]
    smm_bank0 = [bank_free[0], bank_free[1]]
    dec_guard = [t_dgu_dve_last, t_ddn_dve_last] + dq_toks
    for tt in range(NTT):
        pp = tt % 2
        tos = P("sync" if pp == 0 else "scalar", lambda e, tt=tt, pp=pp:
                e.dma_start(
            bass.AP(ostrip_p, pp * 2048, [[2 * 2048, 128], [128, 16], [1, 128]]),
            bass.AP(recv_o.tensor, tt * 128,
                    [[TQ, 128], [128 * TQ, 16], [1, 128]])),
            after=[t_go, ost_free[pp]] + dec_guard, sem=f"os{pp}")
        for cc4 in range(4):
            eb = cc4 % 2
            last = None
            for chi in range(NF):
                a = [tos, woq_tok[chi // 4]]
                if chi == 0:
                    a.append(e_ps_free[eb])
                    a.append(smm_bank0[eb])
                last = P("tensor", lambda e, tt=tt, cc4=cc4, chi=chi, eb=eb,
                         pp=pp: e.matmul(
                    pse[eb],
                    ostrip_p[:, pp * 2048 + chi * 128: pp * 2048 + (chi + 1) * 128],
                    woq[chi // 4][:, (chi % 4) * DM + cc4 * 512:
                                  (chi % 4) * DM + (cc4 + 1) * 512],
                    start=(chi == 0), stop=(chi == NF - 1)),
                    after=a, sem="pe" if chi == NF - 1 else None)
                if cc4 == 3 and chi == NF - 1:
                    ost_free[pp] = last
            telx = P("sync", lambda e, cc4=cc4, tt=tt, eb=eb: e.dma_start(
                xep[:, eb * 512:(eb + 1) * 512],
                x_own[tt * 128:(tt + 1) * 128, cc4 * 512:(cc4 + 1) * 512]),
                after=[xep_free[eb]] + dec_guard, sem=f"xp{eb}")
            te1 = P("vector", lambda e, cc4=cc4, tt=tt, eb=eb: e.tensor_tensor(
                etmp[:, eb * 512:(eb + 1) * 512], pse[eb],
                g_sb[:, tt * DM + cc4 * 512: tt * DM + (cc4 + 1) * 512],
                ALU.mult),
                after=[last, t_g_done.get(tt), etmp_free[eb]] + dec_guard)
            e_ps_free[eb] = te1
            te2 = P("vector", lambda e, eb=eb: e.tensor_tensor(
                etmp[:, eb * 512:(eb + 1) * 512],
                etmp[:, eb * 512:(eb + 1) * 512],
                xep[:, eb * 512:(eb + 1) * 512], ALU.add),
                after=[telx, te1])
            xep_free[eb] = te2
            tst = P("sync", lambda e, cc4=cc4, tt=tt, eb=eb: e.dma_start(
                out_own[tt * 128:(tt + 1) * 128, cc4 * 512:(cc4 + 1) * 512],
                etmp[:, eb * 512:(eb + 1) * 512]), after=[te2], sem=f"ou{eb}")
            etmp_free[eb] = tst
            t_out_last[eb] = tst

    prog.wait("sync", t_out_last)
    prog.emit()
    fctx.close()
    gctx.close()
    sP.close()
    ctx.close()
    return nc


def _in_maps(x, Wq, Wk, Wv, Wo, gupk, gus, dnpk, dns):
    B, S, _ = x.shape
    TQ = S * 2 // 8
    WkT = np.ascontiguousarray(Wk.T)
    WvT = np.ascontiguousarray(Wv.T)
    WoT = np.ascontiguousarray(Wo.T)
    maps = []
    for r in range(8):
        b, g = r // 4, r % 4
        slopes = np.array([2.0 ** (-0.5 * (4 * g + j + 1)) for j in range(4)],
                          np.float64)
        consts = np.zeros((128, 8), np.float32)
        consts[:, 0:4] = (1.0 / (np.sqrt(128.0) * slopes))[None, :].astype(np.float32)
        consts[:, 4:8] = slopes[None, :].astype(np.float32)
        gup_s = np.ascontiguousarray(gupk[:, 32 * r:32 * (r + 1)]).view(np.uint16)
        gup_s = np.ascontiguousarray(
            gup_s.reshape(88, 128, 64).transpose(1, 0, 2).reshape(128, 88 * 64))
        dnp_s = np.ascontiguousarray(dnpk[:, 88 * r:88 * (r + 1)]).view(np.uint16)
        dnp_s = np.ascontiguousarray(
            dnp_s.reshape(16, 128, 176).transpose(1, 0, 2).reshape(128, 16 * 176))
        gusv = np.ascontiguousarray(
            gus[2 * r:2 * r + 2, :].T.reshape(88, 128, 2).transpose(1, 0, 2)
            .reshape(128, 176))
        segidx = (11 * r + np.arange(11)) // 2
        dnsT = dns[segidx, :].T
        dnsv = np.ascontiguousarray(
            dnsT.reshape(16, 128, 11).transpose(1, 0, 2).reshape(128, 176))
        maps.append({
            "x_own": np.ascontiguousarray(x[b, TQ * g:TQ * (g + 1)]),
            "wq_in": np.ascontiguousarray(Wq[512 * g:512 * (g + 1)].T),
            "wk_in": WkT, "wv_in": WvT, "wo_in": WoT,
            "gup": gup_s, "dnp": dnp_s,
            "gusv": gusv, "dnsv": dnsv,
            "cvin": consts,
            "meta": np.array([[256 * g * TQ, TQ * g]], np.int32),
        })
    return maps


def kernel(L0_attn_x, L0_attn_Wq, L0_attn_Wk, L0_attn_Wv, L0_attn_Wo,
           L0_ffn_gate_up_packed, L0_ffn_gate_up_scales,
           L0_ffn_down_packed, L0_ffn_down_scales):
    x = np.ascontiguousarray(np.asarray(L0_attn_x, dtype=np.float32))
    Wq = np.ascontiguousarray(np.asarray(L0_attn_Wq, dtype=np.float32))
    Wk = np.ascontiguousarray(np.asarray(L0_attn_Wk, dtype=np.float32))
    Wv = np.ascontiguousarray(np.asarray(L0_attn_Wv, dtype=np.float32))
    Wo = np.ascontiguousarray(np.asarray(L0_attn_Wo, dtype=np.float32))
    gupk = np.ascontiguousarray(np.asarray(L0_ffn_gate_up_packed, dtype=np.int32))
    gus = np.ascontiguousarray(np.asarray(L0_ffn_gate_up_scales, dtype=np.float16))
    dnpk = np.ascontiguousarray(np.asarray(L0_ffn_down_packed, dtype=np.int32))
    dns = np.ascontiguousarray(np.asarray(L0_ffn_down_scales, dtype=np.float16))

    B, S, _ = x.shape
    TQ = S * 2 // 8
    nc = build_nc(S=S)
    maps = _in_maps(x, Wq, Wk, Wv, Wo, gupk, gus, dnpk, dns)
    res = run_bass_kernel_spmd(nc, maps, core_ids=list(range(8)))
    out = np.empty((B, S, DM), np.float32)
    for r in range(8):
        b, g = r // 4, r % 4
        out[b, TQ * g:TQ * (g + 1)] = np.asarray(res.results[r]["out_own"])
    return out
